# version 18
# baseline (speedup 1.0000x reference)
import math
from contextlib import ExitStack

import numpy as np

import concourse.bass as bass
import concourse.mybir as mybir
from concourse.bass import ds
from concourse.bass_utils import run_bass_kernel_spmd

F32 = mybir.dt.float32
BF16 = mybir.dt.bfloat16
AF = mybir.ActivationFunctionType
ALU = mybir.AluOpType

E = 1024
HD = 32
T = 4096
NT = 32
NQT = 8
NE = 8
LAMBDA_INIT = 0.8 - 0.6 * math.exp(-0.3 * 12)
B_SHIFT = 12.0
SCALE = HD ** -0.5
EPS_LN = 1e-5
EPS_RMS = 1e-6

RR_HEADS = [(0, 3), (1, 2), (4, 7), (5, 6)]

STEPS = []
for _Tq in range(NQT):
    _ns = 4 * (_Tq + 1)
    for _rr in range(4):
        for _half in range(2):
            for _s in range(_ns):
                STEPS.append((_Tq, _rr, _half, _s, _ns))
NSTEP = len(STEPS)
NROUND = NQT * 4 * 2
STEP_ROUND = []
MK_AFTER = []
ROUND_LAST = {}
_r, _key, _mk = -1, None, 0
for _k, (_Tq, _rr, _half, _s, _ns) in enumerate(STEPS):
    if (_Tq, _rr, _half) != _key:
        _key = (_Tq, _rr, _half)
        _r += 1
    STEP_ROUND.append(_r)
    ROUND_LAST[_r] = _k
    if _s >= 4 * _Tq:
        _mk += 1
    MK_AFTER.append(_mk)


def is_diag(k):
    Tq, rr, half, s, ns = STEPS[k]
    return s >= 4 * Tq


NU = NE + 2


def ffn_nic(e):
    return 4


FFN_UNITS = [(tt, e) for tt in range(4) for e in range(NU)]
FFN_IC_BEFORE = {}
IC_END = []
_c = 0
for _i, (_tt, _e) in enumerate(FFN_UNITS):
    for _ic in range(4):
        FFN_IC_BEFORE[(_tt, _e, _ic)] = _c
        _c += 1
    IC_END.append(_c)
FFN_IC_TOTAL = _c
FFN_E_BEFORE = {(tt, e): FFN_IC_BEFORE[(tt, e, 0)] for (tt, e) in FFN_UNITS}
FFN_TT_END = {tt: IC_END[tt * NU + NU - 1] for tt in range(4)}

SLOT_GUARD = [IC_END[_i - 2] if _i >= 2 else 0 for _i in range(len(FFN_UNITS))]

SDFF_AT = {}
_c = 0
for _tt in range(4):
    for _e in range(NU):
        _c += 2
        SDFF_AT[("gu", _tt, _e)] = _c
    _c += 1
    SDFF_AT[("slab", _tt)] = _c
    for _dc in range(8):
        _c += 10
        SDFF_AT[("dn", _tt, _dc)] = _c
SDFF_TOTAL = _c

PT2_BEFORE = {}
_c = 0
for _R in range(NROUND):
    PT2_BEFORE[_R] = _c
    _c += 1
    if _R % 2 == 1:
        _c += 1
PT2_TOTAL = _c

O_QFM = 0
O_KFM = 32 * 1024
O_VTM = 48 * 1024
O_SCR = 64 * 1024 + 256
O_MROW = 128 * 1024
BIGK = 160 * 1024
P_X = O_SCR
P_COS = P_X + 8192
P_SIN = P_COS + 4096
P_XNF = P_SIN + 4096
P_XNB = P_XNF + 4096
P_FMT = P_XNB + 4096
P_RT = P_FMT + 4096
P_QR = P_RT + 4096
P_KR = P_QR + 1024
P_GP = P_KR + 512
P_MBF = P_GP + 64
P_MT = P_MBF + 512
P_WQ = P_MT + 8192
P_WK = P_WQ + 8192
P_WV = P_WK + 4096
P_DUMP = P_WV + 4096
assert P_DUMP + 4096 <= O_MROW
A_OSB = O_SCR
A_DSB = A_OSB + 2048
A_AF = A_DSB + 2048
A_SS = A_AF + 1024
A_RS = A_SS + 512
A_STG = A_RS + 512
A_SQ = A_STG + 2048
assert A_SQ + 1024 <= P_MT
F_SLG = 0
F_SLU = 16 * 1024
F_SLD = 32 * 1024
F_SDS = 48 * 1024
F_WOS = 50 * 1024
F_Z = 52 * 1024
F_ZSH = 84 * 1024
F_SIG = 92 * 1024
F_PS = 94 * 1024
F_ATT = 96 * 1024
F_OFM = 104 * 1024
F_XR = 120 * 1024
F_OSB = 124 * 1024
assert F_OSB + 4096 <= O_MROW


def build_nc(dbg=False):
    nc = bass.Bass("TRN2", target_bir_lowering=False,
                   detect_race_conditions=False)

    x4096 = nc.dram_tensor("x4096", [T, E], F32, kind="ExternalInput").ap()
    x_own = nc.dram_tensor("x_own", [2048, E], F32, kind="ExternalInput").ap()
    wq_eff = nc.dram_tensor("wq_eff", [E, 512], F32, kind="ExternalInput").ap()
    wkv_eff = nc.dram_tensor("wkv_eff", [E, 512], F32,
                             kind="ExternalInput").ap()
    wo_in = nc.dram_tensor("wo_in", [E, E], F32, kind="ExternalInput").ap()
    lamv = nc.dram_tensor("lamv", [1, 128], F32, kind="ExternalInput").ap()
    rmsw_in = nc.dram_tensor("rmsw_in", [1, 64], F32, kind="ExternalInput").ap()
    gwt_in = nc.dram_tensor("gwt_in", [E, 8], F32, kind="ExternalInput").ap()
    lbb = nc.dram_tensor("lbb", [1, 8], F32, kind="ExternalInput").ap()
    egw_t = nc.dram_tensor("egw_t", [NE, E, 512], F32, kind="ExternalInput").ap()
    euw_t = nc.dram_tensor("euw_t", [NE, E, 512], F32, kind="ExternalInput").ap()
    edw_t2 = nc.dram_tensor("edw_t2", [NE, 8, 512, 128], F32,
                            kind="ExternalInput").ap()
    sgw_t = nc.dram_tensor("sgw_t", [E, E], F32, kind="ExternalInput").ap()
    suw_t = nc.dram_tensor("suw_t", [E, E], F32, kind="ExternalInput").ap()
    sdw_t2 = nc.dram_tensor("sdw_t2", [8, E, 128], F32, kind="ExternalInput").ap()
    cosr = nc.dram_tensor("cosr", [T, 512], F32, kind="ExternalInput").ap()
    sinr = nc.dram_tensor("sinr", [T, 512], F32, kind="ExternalInput").ap()
    out_d = nc.dram_tensor("out_d", [2048, E], F32, kind="ExternalOutput").ap()

    egw_b = nc.dram_tensor("egw_b", [NE, E, 512], BF16).ap()
    euw_b = nc.dram_tensor("euw_b", [NE, E, 512], BF16).ap()
    edw_b = nc.dram_tensor("edw_b", [NE, 8, 512, 128], BF16).ap()
    sgw_b = nc.dram_tensor("sgw_b", [E, E], BF16).ap()
    suw_b = nc.dram_tensor("suw_b", [E, E], BF16).ap()
    sdw_b = nc.dram_tensor("sdw_b", [8, E, 128], BF16).ap()
    wo_b = nc.dram_tensor("wo_b", [E, E], BF16).ap()
    xn_d = nc.dram_tensor("xn_d", [8, 128, T], BF16).ap()
    m_dr2 = nc.dram_tensor("m_dr2", [256, 128], BF16).ap()
    dko = dict(kind="ExternalOutput") if dbg == 2 else {}
    attn_my = nc.dram_tensor("attn_my", [512, T], BF16).ap()
    attn_full = nc.dram_tensor("attn_full", [1024, T], BF16).ap()
    cc_in = nc.dram_tensor("cc_in", [1, 16], F32).ap()
    cc_out = nc.dram_tensor("cc_out", [1, 16], F32).ap()
    if dbg:
        dbg_q = nc.dram_tensor("dbg_q", [128, 4 * 4096], BF16, **dko).ap()
        dbg_k = nc.dram_tensor("dbg_k", [128, 2 * 4096], BF16, **dko).ap()
        dbg_v = nc.dram_tensor("dbg_v", [128, 32 * 260], BF16, **dko).ap()
    if dbg == 3:
        dump_gp = nc.dram_tensor("dump_gp", [128, NT * 16], F32,
                                 kind="ExternalOutput").ap()
        dump_gs = nc.dram_tensor("dump_gs", [128, NT * 32], F32,
                                 kind="ExternalOutput").ap()
    if dbg == 2:
        dump_xn = nc.dram_tensor("dump_xn", [128, 8 * 2048], BF16,
                                 kind="ExternalOutput").ap()
        dump_mrow = nc.dram_tensor("dump_mrow", [1, 8 * 2048], BF16,
                                   kind="ExternalOutput").ap()
        dump_cons = nc.dram_tensor("dump_cons", [128, 201], F32,
                                   kind="ExternalOutput").ap()
        dump_z = nc.dram_tensor("dump_z", [128, 20480], BF16,
                                kind="ExternalOutput").ap()
        dump_ofm = nc.dram_tensor("dump_ofm", [128, 4096], F32,
                                  kind="ExternalOutput").ap()
        dump_am = nc.dram_tensor("dump_am", [512, T], BF16,
                                 kind="ExternalOutput").ap()
        dump_af = nc.dram_tensor("dump_af", [1024, T], BF16,
                                 kind="ExternalOutput").ap()

    with ExitStack() as ctx:
        en = ctx.enter_context

        big = en(nc.sbuf_tensor("big", [128, BIGK], mybir.dt.uint8))
        xn_own = en(nc.sbuf_tensor("xn_own", [128, 8 * 2048], BF16))
        p_sb = en(nc.sbuf_tensor("p_sb", [128, 4 * 512], BF16))
        slab = en(nc.sbuf_tensor("slab", [128, 4 * 512], BF16))
        ident_bf = en(nc.sbuf_tensor("ident_bf", [128, 128], BF16))
        ident_f = en(nc.sbuf_tensor("ident_f", [128, 128], F32))
        ones_c_bf = en(nc.sbuf_tensor("ones_c_bf", [128, 1], BF16))
        ones64_bf = en(nc.sbuf_tensor("ones64_bf", [128, 64], BF16))
        ones_c_f = en(nc.sbuf_tensor("ones_c_f", [128, 1], F32))
        ones_r_bf = en(nc.sbuf_tensor("ones_r_bf", [1, 128], BF16))
        ones_r_f = en(nc.sbuf_tensor("ones_r_f", [1, 128], F32))
        lam_col = en(nc.sbuf_tensor("lam_col", [128, 1], F32))
        rmsw_rep = en(nc.sbuf_tensor("rmsw_rep", [128, 64], F32))
        wm_cols = en(nc.sbuf_tensor("wm_cols", [128, 8], F32))
        m_diag = en(nc.sbuf_tensor("m_diag", [128, 4 * 512], BF16))
        gsc = en(nc.sbuf_tensor("gsc", [128, 128], F32))
        lam_sc = en(nc.sbuf_tensor("lam_sc", [1, 128], F32))
        gwt_f = en(nc.sbuf_tensor("gwt_f", [128, 64], F32))
        gwT = en(nc.sbuf_tensor("gwT", [128, 128], BF16))
        lb_row = en(nc.sbuf_tensor("lb_row", [1, 8], F32))
        wmean_r = en(nc.sbuf_tensor("wmean_r", [1, 16], F32))
        c_negB = en(nc.sbuf_tensor("c_negB", [128, 1], F32))
        c_one = en(nc.sbuf_tensor("c_one", [128, 1], F32))
        c_eps5 = en(nc.sbuf_tensor("c_eps5", [128, 1], F32))
        c_eps6 = en(nc.sbuf_tensor("c_eps6", [128, 1], F32))
        nc.const_aps.register(F32, -float(B_SHIFT), c_negB.ap())
        nc.const_aps.register(F32, 1.0, c_one.ap())
        nc.const_aps.register(F32, float(EPS_LN), c_eps5.ap())
        nc.const_aps.register(F32, float(EPS_RMS), c_eps6.ap())

        def V(off, dt, cols, p0=0, np_=128):
            return big[p0:p0 + np_, off:off + cols * mybir.dt.size(dt)].bitcast(dt)

        q_fm = V(O_QFM, BF16, 4 * 4096)
        k_fm = V(O_KFM, BF16, 2 * 4096)
        v_tm = V(O_VTM, BF16, 32 * 260)
        assert O_VTM + 32 * 260 * 2 <= O_SCR
        v_tm4 = v_tm.rearrange("p (c g o) -> p c g o", c=32, g=4)
        m_rows = V(O_MROW, BF16, 8 * 2048, 0, 1)

        sems = {}
        for name in ["sDX", "sDT", "sDW", "sDC", "sDXN", "sDSL", "sDFF", "sDXR",
                     "sDO", "sDM", "sDM2", "sGP", "sPR", "sSQ", "sRST", "sXN",
                     "sXEV", "sPXT", "sPQ", "sRP", "sPT", "sQKE", "sGEX", "sCC",
                     "sDM3", "sTL", "sSC", "sEX", "sMK", "sPV", "sEVC", "sPT2", "sSS", "sST", "sRO",
                     "sRS", "sSTG", "sEPI", "sFG", "sFU", "sFA", "sFZ", "sFM",
                     "sFD", "sFE", "sFT", "sFO", "sDBG", "sDGP", "sVV",
                     "sFP", "sPP"]:
            sems[name] = en(nc.semaphore(name))

        def S(n):
            return sems[n]

        block = en(nc.Block())

        psq = [en(nc.psum_tensor(f"psq{i}", [128, 1024], F32))
               for i in range(2)]
        ps = [psq[0][:, 0:512], psq[0][:, 512:1024],
              psq[1][:, 0:512], psq[1][:, 512:1024]]
        ps += [en(nc.psum_tensor(f"ps{i}", [128, 512], F32))
               for i in range(4, 7)]
        psB = en(nc.psum_tensor("psB", [128, 1024], BF16))
        psBf = psB[:, 0:1024].bitcast(F32)
        ps6bf = ps[6][:, 0:512].bitcast(BF16)
        SCB = {(0, 0): ps[0], (0, 1): ps[1], (1, 0): ps[2], (1, 1): ps[3]}
        PVB = {0: ps[4], 1: ps[5]}
        DNB2 = {0: ps[6], 1: psBf}

        @block.sync
        def _(sy):
            pid = sy.partition_id()
            parity = sy.scalar_reg_alu(ALU.mod, pid, 2)
            tok0 = sy.scalar_reg_alu(ALU.mult, parity, 2048)

            sy.dma_start(out=lam_sc[:], in_=lamv).then_inc(S("sDW"), 16)
            sy.dma_start(out=gsc[0:1, 48:112], in_=rmsw_in).then_inc(S("sDW"), 16)
            sy.dma_start(out=gwt_f[:].rearrange("p (f e) -> p f e", f=8),
                         in_=gwt_in.rearrange("(f p) e -> p f e", p=128)
                         ).then_inc(S("sDW"), 16)
            sy.dma_start(out=lb_row[:], in_=lbb).then_inc(S("sDW"), 16)

            for t in range(NT):
                if t >= 2:
                    sy.wait_ge(S("sXN"), 2 * (t - 1))
                    sy.wait_ge(S("sSQ"), t - 1)
                    sy.wait_ge(S("sRP"), t - 1)
                sy.dma_start(out=V(P_X + (t % 2) * 4096, F32, 1024),
                             in_=x4096[128 * t:128 * t + 128, :]
                             ).then_inc(S("sDX"), 16)
                sy.dma_start(out=V(P_COS + (t % 2) * 2048, F32, 512),
                             in_=cosr[128 * t:128 * t + 128, :]
                             ).then_inc(S("sDT"), 16)
                sy.dma_start(out=V(P_SIN + (t % 2) * 2048, F32, 512),
                             in_=sinr[128 * t:128 * t + 128, :]
                             ).then_inc(S("sDT"), 16)
                if dbg == 3 and t >= 1:
                    sy.wait_ge(S("sRP"), t)
                    sy.dma_start(out=dump_gp[:, 16 * (t - 1):16 * (t - 1) + 16],
                                 in_=V(P_GP, F32, 16)).then_inc(S("sDGP"), 16)
                    sy.dma_start(out=dump_gs[:, 32 * (t - 1):32 * (t - 1) + 32],
                                 in_=gsc[:, 16:48]).then_inc(S("sDGP"), 16)
            if dbg == 3:
                sy.wait_ge(S("sRP"), NT)
                sy.dma_start(out=dump_gp[:, 16 * (NT - 1):16 * NT],
                             in_=V(P_GP, F32, 16)).then_inc(S("sDGP"), 16)
                sy.dma_start(out=dump_gs[:, 32 * (NT - 1):32 * NT],
                             in_=gsc[:, 16:48]).then_inc(S("sDGP"), 16)

            sy.wait_ge(S("sDXN"), 16 * NT)
            sy.dma_start(out=xn_own[:].rearrange("p (f t) -> p f t", f=8),
                         in_=xn_d[:, :, ds(tok0, 2048)]
                             .rearrange("f p t -> p f t")).then_inc(S("sDW"), 16)

            for Tq in range(NQT):
                last_round = (Tq * 4 + 3) * 2 + 1
                sy.wait_ge(S("sEPI"), last_round + 1)
                sy.dma_start(
                    out=attn_my[:, 512 * Tq:512 * Tq + 512]
                        .rearrange("(c p) t -> p c t", p=128),
                    in_=slab[:].rearrange("p (c t) -> p c t", c=4)
                ).then_inc(S("sDSL"), 16)

            sy.wait_ge(S("sDM2"), 16)
            q16 = sy.scalar_reg_alu(ALU.mult, parity, 16)
            m2v = m_dr2.rearrange("(jq e) p -> e jq p", e=8)
            for e in range(8):
                sy.dma_start(out=m_rows[0:1, 2048 * e:2048 * e + 2048]
                             .rearrange("o (q p) -> o q p", q=16),
                             in_=m2v[e:e + 1, ds(q16, 16), :]
                             ).then_inc(S("sDM2"), 16)
            if dbg:
                sy.wait_ge(S("sCC"), 2)
                sy.dma_start(out=dbg_q, in_=q_fm).then_inc(S("sDBG"), 16)
                sy.dma_start(out=dbg_k, in_=k_fm).then_inc(S("sDBG"), 16)
                sy.dma_start(out=dbg_v, in_=v_tm).then_inc(S("sDBG"), 16)
            sy.wait_ge(S("sDC"), 16 * 9)
            sy.wait_ge(S("sDM2"), 16)
            sy.wait_ge(S("sSC"), NSTEP)
            for tt in range(4):
                for e in range(NU):
                    i_u = tt * NU + e
                    if SLOT_GUARD[i_u] > 0:
                        sy.wait_ge(S("sFG"), SLOT_GUARD[i_u])
                    par = i_u % 2
                    if e < NE:
                        gsrc = egw_b[e].rearrange("(f p) o -> p f o", p=128)
                        usrc = euw_b[e].rearrange("(f p) o -> p f o", p=128)
                    else:
                        hh = e - NE
                        gsrc = sgw_b.rearrange("(f p) o -> p f o", p=128)[
                            :, :, 512 * hh:512 * hh + 512]
                        usrc = suw_b.rearrange("(f p) o -> p f o", p=128)[
                            :, :, 512 * hh:512 * hh + 512]
                    sy.dma_start(
                        out=V(F_SLG + par * 8192, BF16, 4096)
                            .rearrange("p (f o) -> p f o", f=8),
                        in_=gsrc).then_inc(S("sDFF"), 16)
                    sy.dma_start(
                        out=V(F_SLU + par * 8192, BF16, 4096)
                            .rearrange("p (f o) -> p f o", f=8),
                        in_=usrc).then_inc(S("sDFF"), 16)
                if tt == 0:
                    sy.wait_ge(S("sCC"), 2)
                    sy.wait_ge(S("sPV"), NSTEP)
                else:
                    sy.wait_ge(S("sFD"), 8 * tt)
                sy.dma_start(
                    out=V(F_ATT, BF16, 8 * 512).rearrange("p (c t) -> p c t", c=8),
                    in_=attn_full.rearrange("(c p) t -> p c t", p=128)
                        [:, :, ds(tok0 + 512 * tt, 512)]).then_inc(S("sDFF"), 16)
                for dc in range(8):
                    idxd = tt * 8 + dc
                    if idxd >= 2:
                        sy.wait_ge(S("sFD"), idxd - 1)
                    par = idxd % 2
                    for e in range(NE):
                        sy.dma_start(
                            out=V(F_SLD + par * 8192 + e * 1024, BF16, 512)
                                .rearrange("p (i o) -> p i o", i=4),
                            in_=edw_b[e, dc].rearrange("(i p) o -> p i o", p=128)
                        ).then_inc(S("sDFF"), 16)
                    if idxd >= 1:
                        sy.wait_ge(S("sFD"), idxd)
                    sy.dma_start(
                        out=V(F_SDS, BF16, 8 * 128)
                            .rearrange("p (i o) -> p i o", i=8),
                        in_=sdw_b[dc].rearrange("(i p) o -> p i o", p=128)
                    ).then_inc(S("sDFF"), 16)
                    sy.dma_start(
                        out=V(F_WOS, BF16, 8 * 128)
                            .rearrange("p (h o) -> p h o", h=8),
                        in_=wo_b.rearrange("(h p) o -> p h o", p=128)
                            [:, :, 128 * dc:128 * dc + 128]).then_inc(S("sDFF"), 16)
                for j in range(4):
                    idxr = tt * 4 + j
                    if idxr >= 1:
                        sy.wait_ge(S("sFO"), idxr)
                    sy.dma_start(
                        out=V(F_XR, F32, 1024),
                        in_=x_own[512 * tt + 128 * j:512 * tt + 128 * j + 128, :]
                    ).then_inc(S("sDXR"), 16)
                    sy.wait_ge(S("sFO"), idxr + 1)
                    sy.dma_start(
                        out=out_d[512 * tt + 128 * j:512 * tt + 128 * j + 128, :],
                        in_=V(F_OSB, F32, 1024)
                    ).then_inc(S("sDO"), 16)

            sy.wait_ge(S("sDO"), 16 * 16)
            if dbg == 2:
                sy.dma_start(out=dump_xn, in_=xn_own[:]).then_inc(S("sDBG"), 16)
                sy.dma_start(out=dump_mrow,
                             in_=m_rows[0:1, :]).then_inc(S("sDBG"), 16)
                with nc.allow_non_contiguous_dma(reason="debug dumps"):
                    sy.dma_start(out=dump_cons[:, 0:1],
                                 in_=lam_col[:]).then_inc(S("sDBG"), 16)
                    sy.dma_start(out=dump_cons[:, 1:65],
                                 in_=rmsw_rep[:]).then_inc(S("sDBG"), 16)
                    sy.dma_start(out=dump_cons[:, 65:73],
                                 in_=wm_cols[:]).then_inc(S("sDBG"), 16)
                    sy.dma_start(out=dump_cons[:, 73:201],
                                 in_=gsc[:]).then_inc(S("sDBG"), 16)
                sy.dma_start(out=dump_z,
                             in_=V(F_Z, BF16, 20480)).then_inc(S("sDBG"), 16)
                sy.dma_start(out=dump_ofm,
                             in_=V(F_OFM, F32, 4096)).then_inc(S("sDBG"), 16)
                sy.dma_start(out=dump_am, in_=attn_my).then_inc(S("sDBG"), 16)
                sy.dma_start(out=dump_af, in_=attn_full).then_inc(S("sDBG"), 16)
                sy.wait_ge(S("sDBG"), 48 + 16 * 10)

        @block.gpsimd
        def _(g):
            pid = g.partition_id()
            parity = g.scalar_reg_alu(ALU.mod, pid, 2)
            tok0g = g.scalar_reg_alu(ALU.mult, parity, 2048)

            g.wait_ge(S("sPR"), 1)
            g.affine_select(ident_bf[:], ident_bf[:], [[1, 128]], ALU.is_equal,
                            0.0, base=0, channel_multiplier=-1
                            ).then_inc(S("sGP"), 1)
            g.affine_select(ident_f[:], ident_f[:], [[1, 128]], ALU.is_equal,
                            0.0, base=0, channel_multiplier=-1
                            ).then_inc(S("sGP"), 1)
            for dd in range(4):
                g.affine_select(
                    m_diag[:, 512 * dd:512 * dd + 512],
                    m_diag[:, 512 * dd:512 * dd + 512],
                    [[1, 512]], ALU.is_ge, 0.0,
                    base=-128 * dd, channel_multiplier=-1
                ).then_inc(S("sGP"), 1)

            g.dma_start(out=V(P_WQ, BF16, 8 * 512)
                        .rearrange("p (f n) -> p f n", f=8),
                        in_=wq_eff.rearrange("(f p) n -> p f n", p=128)
                        ).then_inc(S("sDC"), 16)
            g.dma_start(out=V(P_WK, BF16, 8 * 512)
                        .rearrange("p (f n) -> p f n", f=8),
                        in_=wkv_eff.rearrange("(f p) n -> p f n", p=128)
                        ).then_inc(S("sDC"), 16)
            g.wait_ge(S("sRP"), 8)
            for dst, src in [(egw_b, egw_t), (euw_b, euw_t), (edw_b, edw_t2),
                             (sgw_b, sgw_t), (suw_b, suw_t), (sdw_b, sdw_t2),
                             (wo_b, wo_in)]:
                g.dma_start(out=dst, in_=src).then_inc(S("sDC"), 16)

            g.wait_ge(S("sTL"), 2)
            g.dma_start(out=cc_in, in_=gsc[0:1, 112:128]).then_inc(S("sDM3"), 16)
            g.wait_ge(S("sDM3"), 16)
            g.collective_compute(
                "AllReduce", ALU.add, replica_groups=[list(range(8))],
                ins=[cc_in], outs=[cc_out]).then_inc(S("sCC"), 1)
            g.wait_ge(S("sCC"), 1)
            g.dma_start(out=wmean_r[:], in_=cc_out).then_inc(S("sDM3"), 16)

            g.wait_ge(S("sDSL"), 16 * NQT)
            g.collective_compute(
                "AllGather", ALU.bypass,
                replica_groups=[[0, 1], [2, 3], [4, 5], [6, 7]],
                ins=[attn_my], outs=[attn_full]).then_inc(S("sCC"), 1)

        @block.scalar
        def _(s):
            s.wait_ge(S("sPR"), 1)
            for t in range(NT):
                s.wait_ge(S("sDX"), 16 * (t + 1))
                if t >= 1:
                    s.wait_ge(S("sXN"), 2 * t - 1)
                xv = V(P_X + (t % 2) * 4096, F32, 1024)
                s.activation(V(P_DUMP, F32, 1024), xv, AF.Square,
                             accum_out=gsc[:, 1:2]).then_inc(S("sSQ"), 1)
                s.wait_ge(S("sXN"), 2 * t + 1)
                s.activation(gsc[:, 4:5], gsc[:, 3:4], AF.Ln, bias=float(EPS_LN))
                s.activation(gsc[:, 5:6], gsc[:, 4:5], AF.Exp,
                             scale=-0.5).then_inc(S("sRST"), 1)
                s.wait_ge(S("sPQ"), t + 1)
                if t >= 1:
                    s.wait_ge(S("sRP"), t)
                s.activation(gsc[:, 8:16], ps[6][:, 8 * (t % 2):8 * (t % 2) + 8],
                             AF.Sigmoid).then_inc(S("sGEX"), 1)
                s.wait_ge(S("sXEV"), t + 1)
                s.dma_start(
                    out=xn_d[:, :, 128 * t:128 * t + 128]
                        .rearrange("f p t -> p f t"),
                    in_=V(P_FMT + (t % 2) * 2048, BF16, 1024)
                        .rearrange("p (f t) -> p f t", f=8)).then_inc(S("sDXN"), 16)

            s.wait_ge(S("sTL"), 3)
            s.activation(gsc[0:1, 11:12], gsc[0:1, 9:10], AF.Exp)
            s.activation(gsc[0:1, 12:13], gsc[0:1, 10:11],
                         AF.Exp).then_inc(S("sRST"), 1)
            s.wait_ge(S("sDM"), 2)
            s.dma_start(out=m_dr2.rearrange("(j q) p -> q j p", j=2),
                        in_=V(P_MT, BF16, 256)
                        .rearrange("q (j p) -> q j p", j=2)
                        ).then_inc(S("sDM2"), 16)

            pending = []
            for k in range(NSTEP):
                Tq, rr, half, sch, ns = STEPS[k]
                s.wait_ge(S("sSC"), k + 1)
                if k >= 2:
                    s.wait_ge(S("sPV"), k - 1)
                par = k % 2
                d = sch - 4 * Tq
                if d <= 0:
                    ins = s.activation(
                        p_sb[:, par * 1024:par * 1024 + 1024],
                        psq[par][:, 0:1024],
                        AF.Exp, bias=-float(B_SHIFT), scale=float(SCALE))
                else:
                    for u in range(2):
                        c0 = (par * 2 + u) * 512
                        ins = s.activation(
                            p_sb[:, c0 + 128 * d:c0 + 512],
                            psq[par][:, 512 * u + 128 * d:512 * u + 512],
                            AF.Exp, bias=-float(B_SHIFT), scale=float(SCALE))
                ins.then_inc(S("sEX"), 1)
                if sch == ns - 1:
                    pending.append(STEP_ROUND[k])
                flush = [R for R in pending if ROUND_LAST[R] <= k - 1]
                if k == NSTEP - 1:
                    flush = pending[:]
                for R in flush:
                    pending.remove(R)
                    s.wait_ge(S("sSS"), R + 1)
                    s.activation(V(A_RS, F32, 4), V(A_SS, F32, 4), AF.Ln,
                                 bias=float(EPS_RMS), scale=1.0 / 64.0)
                    s.activation(V(A_RS, F32, 4), V(A_RS, F32, 4), AF.Exp,
                                 scale=-0.5).then_inc(S("sRS"), 1)

            for tt in range(4):
                for e in range(NU):
                    for ic in range(4):
                        cnt = FFN_IC_BEFORE[(tt, e, ic)]
                        s.wait_ge(S("sFG"), cnt + 1)
                        if cnt >= 1:
                            s.wait_ge(S("sFP"), cnt)
                        sig = V(F_SIG, F32, 512)
                        s.activation(sig, ps[0][:, 0:512],
                                     AF.Silu).then_inc(S("sFA"), 1)

        @block.vector
        def _(v):
            v.memset(ones_c_bf[:], 1.0)
            v.memset(ones64_bf[:], 1.0)
            v.memset(ones_c_f[:], 1.0)
            v.memset(ones_r_bf[:], 1.0)
            v.memset(ones_r_f[:], 1.0)
            v.memset(c_negB[:], -float(B_SHIFT))
            v.memset(c_one[:], 1.0)
            v.memset(c_eps5[:], float(EPS_LN))
            v.memset(ident_bf[:], 1.0)
            v.memset(ident_f[:], 1.0)
            v.memset(m_diag[:], 1.0)
            v.memset(v_tm4[:, :, :, 64:65], 1.0)
            v.memset(c_eps6[:], float(EPS_RMS)).then_inc(S("sPR"), 1)

            vvc = [0]

            def VI(ins):
                vvc[0] += 1
                ins.then_inc(S("sVV"), 1)

            def VW():
                v.wait_ge(S("sVV"), vvc[0])

            v.wait_ge(S("sDW"), 64)
            glo = V(P_RT, F32, 64)
            VI(v.tensor_copy(gwT[:, 0:64], gwt_f[:, 0:64]))
            VW()
            VI(v.tensor_copy(glo, gwT[:, 0:64]))
            VW()
            VI(v.tensor_tensor(glo, gwt_f[:, 0:64], glo, ALU.subtract))
            VW()
            v.tensor_copy(gwT[:, 64:128], glo).then_inc(S("sTL"), 1)

            for t in range(NT):
                s_x = V(P_X + (t % 2) * 4096, F32, 1024)
                v.wait_ge(S("sDX"), 16 * (t + 1))
                VI(v.reduce_sum(gsc[:, 0:1], s_x, mybir.AxisListType.X))
                VW()
                VI(v.scalar_tensor_tensor(gsc[:, 6:7], gsc[:, 0:1],
                                          1.0 / (1024.0 * 1024.0), gsc[:, 0:1],
                                          ALU.mult, ALU.mult))
                VI(v.tensor_scalar(gsc[:, 2:3], gsc[:, 0:1], 1.0 / 1024.0,
                                   None, ALU.mult))
                v.wait_ge(S("sSQ"), t + 1)
                VW()
                v.scalar_tensor_tensor(gsc[:, 3:4], gsc[:, 1:2], 1.0 / 1024.0,
                                       gsc[:, 6:7], ALU.mult,
                                       ALU.subtract).then_inc(S("sXN"), 1)
                v.wait_ge(S("sRST"), t + 1)
                if t >= 2:
                    v.wait_ge(S("sPXT"), t - 1)
                v.tensor_scalar(V(P_XNB + (t % 2) * 2048, BF16, 1024),
                                s_x, gsc[:, 2:3], gsc[:, 5:6],
                                ALU.subtract, ALU.mult).then_inc(S("sXN"), 1)
                v.wait_ge(S("sPXT"), t + 1)
                if t >= 2:
                    v.wait_ge(S("sPQ"), t - 1)
                    v.wait_ge(S("sDXN"), 16 * (t - 1))
                v.tensor_copy(V(P_FMT + (t % 2) * 2048, BF16, 1024),
                              psB[:, 0:1024]).then_inc(S("sXEV"), 1)
                v.wait_ge(S("sPQ"), t + 1)
                v.wait_ge(S("sDT"), 16 * (2 * t + 2))
                if t >= 1:
                    v.wait_ge(S("sPT"), t)
                cosv = V(P_COS + (t % 2) * 2048, F32, 512)
                sinv = V(P_SIN + (t % 2) * 2048, F32, 512)
                rt0 = V(P_RT, F32, 512)
                rt1 = V(P_RT + 2048, F32, 512)
                pqkv = psq[t % 2]
                q3 = pqkv[:, 0:512].rearrange("p (c i) -> p c i", c=16)
                s3 = sinv.rearrange("p (c i) -> p c i", c=16)
                r13 = rt1.rearrange("p (c i) -> p c i", c=16)
                VI(v.tensor_tensor(rt0, pqkv[:, 0:512], cosv, ALU.mult))
                VI(v.tensor_tensor(r13[:, :, 0:16], q3[:, :, 16:32],
                                   s3[:, :, 0:16], ALU.mult))
                VI(v.tensor_tensor(r13[:, :, 16:32], q3[:, :, 0:16],
                                   s3[:, :, 16:32], ALU.mult))
                v.tensor_copy(v_tm4[:, t, :, 0:64],
                              pqkv[:, 768:1024].rearrange("p (g o) -> p g o",
                                                          g=4))
                VW()
                VI(v.tensor_tensor(V(P_QR, BF16, 512), rt0, rt1, ALU.add))
                k3 = pqkv[:, 512:768].rearrange("p (c i) -> p c i", c=8)
                VW()
                VI(v.tensor_tensor(rt0[:, 0:256], pqkv[:, 512:768],
                                   cosv[:, 0:256], ALU.mult))
                VI(v.tensor_tensor(r13[:, 0:8, 0:16], k3[:, :, 16:32],
                                   s3[:, 0:8, 0:16], ALU.mult))
                VI(v.tensor_tensor(r13[:, 0:8, 16:32], k3[:, :, 0:16],
                                   s3[:, 0:8, 16:32], ALU.mult))
                v.wait_ge(S("sGEX"), t + 1)
                if dbg == 3 and t >= 1:
                    v.wait_ge(S("sDGP"), 32 * t)
                e8 = gsc[:, 8:16]
                gp = V(P_GP, F32, 16)
                qe = [q_fm[:, b * 4096 + 128 * t:b * 4096 + 128 * t + 128]
                      for b in range(4)]
                ke = [k_fm[:, b * 4096 + 128 * t:b * 4096 + 128 * t + 128]
                      for b in range(2)]
                VW()
                v.tensor_tensor(V(P_KR, BF16, 256), rt0[:, 0:256],
                                rt1[:, 0:256], ALU.add).then_inc(S("sRO"), 1)
                v.wait_ge(S("sPT"), t + 1)
                v.tensor_copy(qe[0], ps6bf[:, 32:160])
                VI(v.reduce_max(gsc[:, 32:33], e8, mybir.AxisListType.X))
                v.tensor_copy(qe[1], ps6bf[:, 160:288])
                VW()
                VI(v.tensor_scalar(gsc[:, 16:24], e8, gsc[:, 32:33], None,
                                   ALU.is_ge))
                v.tensor_copy(qe[2], ps6bf[:, 288:416])
                VW()
                VI(v.reduce_sum(gsc[:, 33:34], gsc[:, 16:24],
                                mybir.AxisListType.X))
                VI(v.scalar_tensor_tensor(gsc[:, 24:32], gsc[:, 16:24],
                                          -1e30, e8, ALU.mult, ALU.add))
                v.tensor_copy(qe[3], ps6bf[:, 416:544])
                VW()
                VI(v.reduce_max(gsc[:, 34:35], gsc[:, 24:32],
                                mybir.AxisListType.X))
                VI(v.tensor_scalar(gsc[:, 35:36], gsc[:, 33:34], 1.5, None,
                                   ALU.is_ge))
                v.tensor_copy(ke[0], ps6bf[:, 544:672])
                VW()
                VI(v.tensor_tensor(gsc[:, 36:37], gsc[:, 32:33],
                                   gsc[:, 34:35], ALU.subtract))
                VW()
                VI(v.scalar_tensor_tensor(gsc[:, 37:38], gsc[:, 35:36],
                                          gsc[:, 36:37], gsc[:, 34:35],
                                          ALU.mult, ALU.add))
                v.tensor_copy(ke[1], ps6bf[:, 672:800]).then_inc(S("sQKE"), 1)
                VW()
                VI(v.tensor_scalar(gp[:, 0:8], e8, gsc[:, 37:38], None,
                                   ALU.is_ge))
                VW()
                VI(v.tensor_tensor(gsc[:, 24:32], e8, gp[:, 0:8], ALU.mult))
                v.tensor_copy(V(P_MBF, BF16, NT * 8)[:, 8 * t:8 * t + 8],
                              gp[:, 0:8])
                VW()
                VI(v.reduce_sum(gsc[:, 38:39], gsc[:, 24:32],
                                mybir.AxisListType.X))
                VW()
                VI(v.tensor_scalar(gsc[:, 38:39], gsc[:, 38:39], 1e-8,
                                   None, ALU.add))
                VW()
                VI(v.reciprocal(gsc[:, 39:40], gsc[:, 38:39]))
                VW()
                v.tensor_scalar(gp[:, 8:16], gsc[:, 24:32], gsc[:, 39:40],
                                None, ALU.mult).then_inc(S("sRP"), 1)

            v.wait_ge(S("sST"), NT)
            v.tensor_copy(gsc[0:1, 112:128],
                          ps[4][0:1, 0:16]).then_inc(S("sTL"), 1)
            v.wait_ge(S("sDW"), 16)
            if dbg == 3:
                v.wait_ge(S("sDGP"), 32 * NT)
            VI(v.tensor_tensor(gsc[0:1, 16:48], lam_sc[0:1, 0:32],
                               lam_sc[0:1, 32:64], ALU.mult))
            VW()
            VI(v.reduce_sum(gsc[0:1, 9:10], gsc[0:1, 16:48],
                            mybir.AxisListType.X))
            VW()
            VI(v.tensor_tensor(gsc[0:1, 16:48], lam_sc[0:1, 64:96],
                               lam_sc[0:1, 96:128], ALU.mult))
            VW()
            v.reduce_sum(gsc[0:1, 10:11], gsc[0:1, 16:48],
                         mybir.AxisListType.X).then_inc(S("sTL"), 1)
            v.wait_ge(S("sRST"), NT + 1)
            VI(v.scalar_tensor_tensor(gsc[0:1, 14:15], gsc[0:1, 11:12],
                                      float(LAMBDA_INIT), gsc[0:1, 12:13],
                                      ALU.add, ALU.subtract))
            v.tensor_scalar(gsc[0:1, 48:112], gsc[0:1, 48:112],
                            float(1.0 - LAMBDA_INIT), None, ALU.mult)
            VW()
            v.tensor_scalar(gsc[0:1, 15:16], gsc[0:1, 14:15], -1.0, None,
                            ALU.mult).then_inc(S("sTL"), 1)
            v.wait_ge(S("sDM3"), 32)
            VI(v.tensor_scalar(gsc[0:1, 40:48], wmean_r[0:1, 0:8], 1.0, None,
                               ALU.max))
            VW()
            VI(v.reciprocal(gsc[0:1, 0:8], gsc[0:1, 40:48]))
            VW()
            v.tensor_tensor(gsc[0:1, 32:40], wmean_r[0:1, 8:16],
                            gsc[0:1, 0:8], ALU.mult).then_inc(S("sTL"), 1)
            v.wait_ge(S("sPT"), NT + 1)
            v.tensor_copy(lam_col[:], ps[5][:, 0:1])
            v.tensor_copy(rmsw_rep[:], ps[5][:, 64:128])
            v.tensor_copy(wm_cols[:], ps[5][:, 128:136]).then_inc(S("sTL"), 1)

            v.wait_ge(S("sGP"), 6)
            v.wait_ge(S("sDM"), 1)
            v.tensor_copy(V(P_MT, BF16, 256), psB[:, 0:256]).then_inc(S("sDM"), 1)
            for k in range(NSTEP):
                Tq, rr, half, sch, ns = STEPS[k]
                par = k % 2
                R = STEP_ROUND[k]
                if is_diag(k):
                    v.wait_ge(S("sEX"), k + 1)
                    dd = sch - 4 * Tq
                    ins = None
                    for u in range(2):
                        c0 = (par * 2 + u) * 512 + 128 * dd
                        sl = p_sb[:, c0:(par * 2 + u) * 512 + 512]
                        ins = v.tensor_tensor(
                            sl, sl,
                            m_diag[:, 512 * dd + 128 * dd:512 * dd + 512],
                            ALU.mult)
                    ins.then_inc(S("sMK"), 1)
                if sch == ns - 1:
                    v.wait_ge(S("sPV"), k + 1)
                    osb = V(A_OSB, F32, 512)
                    dsb = V(A_DSB, F32, 512)
                    v.tensor_copy(V(A_OSB, F32, 512, 0, 65),
                                  PVB[R % 2][0:65, 0:512])
                    v.tensor_copy(V(A_DSB, F32, 512, 0, 65),
                                  DNB2[R % 2][0:65, 0:512]).then_inc(S("sEVC"), 1)
                    v.wait_ge(S("sPT2"), PT2_BEFORE[R] + 1)
                    af = V(A_AF, F32, 4 * 64)
                    ss = V(A_SS, F32, 4)
                    sqv = V(A_SQ, F32, 4 * 64)
                    for j in range(4):
                        v.reciprocal(gsc[:, 96 + 2 * j:97 + 2 * j],
                                     PVB[R % 2][:, 128 * j + 64:128 * j + 65])
                    for j in range(4):
                        v.reciprocal(gsc[:, 97 + 2 * j:98 + 2 * j],
                                     DNB2[R % 2][:, 128 * j + 64:128 * j + 65])
                    for j in range(4):
                        v.tensor_scalar(gsc[:, 104 + j:105 + j],
                                        gsc[:, 97 + 2 * j:98 + 2 * j],
                                        lam_col[:, 0:1], None, ALU.mult)
                    for j in range(4):
                        oT = PVB[R % 2][:, 128 * j:128 * j + 128]
                        v.tensor_scalar(af[:, 64 * j:64 * j + 64], oT[:, 0:64],
                                        gsc[:, 96 + 2 * j:97 + 2 * j], None,
                                        ALU.mult)
                    for j in range(4):
                        oT2 = DNB2[R % 2][:, 128 * j:128 * j + 128]
                        v.scalar_tensor_tensor(
                            af[:, 64 * j:64 * j + 64], oT2[:, 0:64],
                            gsc[:, 104 + j:105 + j], af[:, 64 * j:64 * j + 64],
                            ALU.mult, ALU.add)
                    for j in range(4):
                        v.tensor_tensor(sqv[:, 64 * j:64 * j + 64],
                                        af[:, 64 * j:64 * j + 64],
                                        af[:, 64 * j:64 * j + 64], ALU.mult)
                    ins = None
                    for j in range(4):
                        ins = v.reduce_sum(ss[:, j:j + 1],
                                           sqv[:, 64 * j:64 * j + 64],
                                           mybir.AxisListType.X)
                    VI(ins)
                    VW()
                    v.tensor_copy(gsc[:, 80:81], ss[:, 0:1]).then_inc(S("sSS"), 1)
                    v.wait_ge(S("sRS"), R + 1)
                    rs = V(A_RS, F32, 4)
                    stg = V(A_STG, F32, 4 * 128)
                    ja, jb = RR_HEADS[rr]
                    j_head = ja if half == 0 else jb
                    colh = 0 if (j_head % 2 == 0) else 64
                    ins = None
                    for j in range(4):
                        v.tensor_scalar(af[:, 64 * j:64 * j + 64],
                                        af[:, 64 * j:64 * j + 64],
                                        rs[:, j:j + 1], None, ALU.mult)
                    for j in range(4):
                        ins = v.tensor_tensor(
                            stg[:, 128 * j + colh:128 * j + colh + 64],
                            af[:, 64 * j:64 * j + 64], rmsw_rep[:, 0:64],
                            ALU.mult)
                    if half == 0:
                        ins.then_inc(S("sSTG"), 1)
                        v.tensor_copy(gsc[:, 80:81],
                                      rs[:, 0:1]).then_inc(S("sEPI"), 1)
                    else:
                        ins.then_inc(S("sSTG"), 1)
                        v.wait_ge(S("sPT2"), PT2_BEFORE[R] + 2)
                        if Tq >= 1:
                            v.wait_ge(S("sDSL"), 16 * Tq)
                        ch_even = (ja if ja % 2 == 0 else jb) // 2
                        ch_odd = (ja if ja % 2 == 1 else jb) // 2
                        bk = PVB[R % 2]
                        ins = None
                        for j in range(4):
                            v.tensor_copy(
                                slab[0:64, ch_even * 512 + 128 * j:
                                     ch_even * 512 + 128 * j + 128],
                                bk[0:64, 128 * j:128 * j + 128])
                            ins = v.tensor_copy(
                                slab[64:128, ch_odd * 512 + 128 * j:
                                     ch_odd * 512 + 128 * j + 128],
                                bk[64:128, 128 * j:128 * j + 128])
                        ins.then_inc(S("sEPI"), 1)

            v.wait_ge(S("sPT2"), PT2_TOTAL)
            for tt in range(4):
                for e in range(NU):
                    zoff = F_Z + e * 4096 if e < NE else F_ZSH + (e - NE) * 4096
                    for ic in range(4):
                        cnt = FFN_IC_BEFORE[(tt, e, ic)]
                        v.wait_ge(S("sFA"), cnt + 1)
                        v.wait_ge(S("sFU"), cnt + 1)
                        if tt >= 1 and ic == 0:
                            v.wait_ge(S("sFD"), 8 * tt)
                        sig = V(F_SIG, F32, 512)
                        p1 = V(F_PS, F32, 512)
                        v.tensor_tensor(p1, sig, ps[1][:, 0:512],
                                        ALU.mult).then_inc(S("sFP"), 1)
                        v.wait_ge(S("sFP"), cnt + 1)
                        if e < NE:
                            v.wait_ge(S("sFM"), tt * NE + e + 1)
                            v.scalar_tensor_tensor(
                                V(zoff + ic * 1024, BF16, 512), p1,
                                wm_cols[:, e:e + 1], ps[2][:, 0:512],
                                ALU.mult, ALU.mult).then_inc(S("sFZ"), 1)
                        else:
                            v.tensor_copy(V(zoff + ic * 1024, BF16, 512),
                                          p1).then_inc(S("sFZ"), 1)
                for dc in range(8):
                    idxd = tt * 8 + dc
                    v.wait_ge(S("sFD"), idxd + 1)
                    v.tensor_copy(V(F_OFM + dc * 2048, F32, 512),
                                  ps[3][:, 0:512]).then_inc(S("sFE"), 1)
                for j in range(4):
                    idxr = tt * 4 + j
                    v.wait_ge(S("sFT"), idxr + 1)
                    v.wait_ge(S("sDXR"), 16 * (idxr + 1))
                    if idxr >= 1:
                        v.wait_ge(S("sDO"), 16 * idxr)
                    xr = V(F_XR, F32, 1024)
                    ob = V(F_OSB, F32, 1024)
                    v.tensor_tensor(ob[:, 0:512], ps[4][:, 0:512],
                                    xr[:, 0:512], ALU.add)
                    v.tensor_tensor(ob[:, 512:1024], ps[5][:, 0:512],
                                    xr[:, 512:1024], ALU.add).then_inc(S("sFO"), 1)

        @block.tensor
        def _(t_):
            t_.wait_ge(S("sGP"), 6)
            t_.wait_ge(S("sPR"), 1)

            def emit_qkT(tq):
                t_.wait_ge(S("sRO"), tq + 1)
                if tq >= 1:
                    t_.wait_ge(S("sQKE"), tq)
                qr = V(P_QR, BF16, 512)
                kr = V(P_KR, BF16, 256)
                ins = None
                for b in range(4):
                    ins = t_.transpose(ps6bf[:, 32 + 128 * b:160 + 128 * b],
                                       qr[:, 128 * b:128 * b + 128],
                                       ident_bf[:])
                for b in range(2):
                    ins = t_.transpose(ps6bf[:, 544 + 128 * b:672 + 128 * b],
                                       kr[:, 128 * b:128 * b + 128],
                                       ident_bf[:])
                ins.then_inc(S("sPT"), 1)

            def emit_stats(tq):
                t_.wait_ge(S("sRP"), tq + 1)
                t_.matmul(ps[4][0:1, 0:16], ones_c_f[:], V(P_GP, F32, 16),
                          start=(tq == 0), stop=(tq == NT - 1),
                          skip_group_check=True).then_inc(S("sST"), 1)

            for t in range(NT):
                pt = t % 2
                if t >= 2:
                    emit_stats(t - 2)
                if t >= 1:
                    emit_qkT(t - 1)
                t_.wait_ge(S("sXN"), 2 * t + 2)
                if t >= 1:
                    t_.wait_ge(S("sXEV"), t)
                xnb = V(P_XNB + pt * 2048, BF16, 1024)
                ins = None
                for fc in range(8):
                    ins = t_.transpose(psB[:, 128 * fc:128 * fc + 128],
                                       xnb[:, 128 * fc:128 * fc + 128],
                                       ident_bf[:])
                ins.then_inc(S("sPXT"), 1)
                t_.wait_ge(S("sXEV"), t + 1)
                if t >= 2:
                    t_.wait_ge(S("sRO"), t - 1)
                    t_.wait_ge(S("sGEX"), t - 1)
                if t == 0:
                    t_.wait_ge(S("sDC"), 16 * 2)
                    t_.wait_ge(S("sDW"), 64)
                    t_.wait_ge(S("sTL"), 1)
                fmv = V(P_FMT + pt * 2048, BF16, 1024)
                wqv = V(P_WQ, BF16, 8 * 512)
                wkv = V(P_WK, BF16, 8 * 512)
                pg = ps[6][:, 8 * pt:8 * pt + 8]
                for fc in range(8):
                    lhs = fmv[:, 128 * fc:128 * fc + 128]
                    t_.matmul(psq[pt][:, 0:512], lhs,
                              wqv[:, 512 * fc:512 * fc + 512],
                              start=(fc == 0), stop=(fc == 7),
                              skip_group_check=True)
                    t_.matmul(psq[pt][:, 512:1024], lhs,
                              wkv[:, 512 * fc:512 * fc + 512],
                              start=(fc == 0), stop=(fc == 7),
                              skip_group_check=True)
                    t_.matmul(pg, lhs, gwT[:, 8 * fc:8 * fc + 8],
                              start=(fc == 0), stop=False, skip_group_check=True)
                    t_.matmul(pg, lhs, gwT[:, 64 + 8 * fc:64 + 8 * fc + 8],
                              start=False, stop=False, skip_group_check=True)
                t_.matmul(pg, ones_r_f[:], lb_row[:],
                          start=False, stop=True,
                          skip_group_check=True).then_inc(S("sPQ"), 1)
            emit_stats(NT - 2)
            emit_qkT(NT - 1)
            emit_stats(NT - 1)

            t_.wait_ge(S("sRP"), NT)
            t_.wait_ge(S("sQKE"), NT)
            ins = None
            for j in range(2):
                ins = t_.transpose(psB[:, 128 * j:128 * j + 128],
                                   V(P_MBF, BF16, 256)[:, 128 * j:128 * j + 128],
                                   ident_bf[:])
            ins.then_inc(S("sDM"), 1)

            t_.wait_ge(S("sTL"), 5)
            t_.matmul(ps[5][:, 0:1], ones_r_f[:], gsc[0:1, 15:16],
                      start=True, stop=True, skip_group_check=True)
            t_.matmul(ps[5][:, 64:128], ones_r_f[:], gsc[0:1, 48:112],
                      start=True, stop=True, skip_group_check=True)
            t_.matmul(ps[5][:, 128:136], ones_r_f[:], gsc[0:1, 32:40],
                      start=True, stop=True,
                      skip_group_check=True).then_inc(S("sPT"), 1)

            pending_bt = []

            def emit_score(j):
                Tqj, rrj, halfj, schj, nsj = STEPS[j]
                parj = j % 2
                if j >= 1:
                    t_.wait_ge(S("sEX"), j - 1)
                if j < 2:
                    t_.wait_ge(S("sRP"), NT)
                    t_.wait_ge(S("sQKE"), NT)
                    t_.wait_ge(S("sGEX"), NT)
                kbj = rrj // 2
                q0 = 128 * max(schj - 4 * Tqj, 0)
                ins = None
                for u in range(2):
                    base = 64 * halfj + 32 * u
                    ins = t_.matmul(
                        SCB[(parj, u)][:, q0:512],
                        k_fm[base:base + 32,
                             kbj * 4096 + 128 * schj:
                             kbj * 4096 + 128 * schj + 128],
                        q_fm[base:base + 32,
                             rrj * 4096 + 512 * Tqj + q0:
                             rrj * 4096 + 512 * Tqj + 512],
                        start=True, stop=True, tile_position=(base, 0),
                        skip_group_check=True)
                ins.then_inc(S("sSC"), 1)

            emit_score(0)
            for k in range(NSTEP):
                Tq, rr, half, sch, ns = STEPS[k]
                par = k % 2
                R = STEP_ROUND[k]
                if k + 1 < NSTEP:
                    emit_score(k + 1)
                for Rbt in pending_bt:
                    t_.wait_ge(S("sSTG"), Rbt + 1)
                    stg = V(A_STG, F32, 4 * 128)
                    ins2 = None
                    for j in range(4):
                        ins2 = t_.transpose(
                            PVB[Rbt % 2][:, 128 * j:128 * j + 128],
                            stg[:, 128 * j:128 * j + 128], ident_f[:])
                    ins2.then_inc(S("sPT2"), 1)
                pending_bt = []
                t_.wait_ge(S("sEX"), k + 1)
                if is_diag(k):
                    t_.wait_ge(S("sMK"), MK_AFTER[k])
                if sch == 0:
                    if R >= 2:
                        t_.wait_ge(S("sSS"), R - 1)
                        t_.wait_ge(S("sEPI"), R - 1)
                    if R < 2:
                        t_.wait_ge(S("sTL"), 6)
                    if R == 1:
                        t_.wait_ge(S("sDM"), 2)
                ja, jb = RR_HEADS[rr]
                j_head = ja if half == 0 else jb
                gloc = j_head // 2
                st = (sch == 0)
                sp = (sch == ns - 1)
                q0 = 128 * max(sch - 4 * Tq, 0)
                banks = [PVB[R % 2], DNB2[R % 2]]
                ins = None
                for u in range(2):
                    psl = p_sb[:, (par * 2 + u) * 512 + q0:
                               (par * 2 + u) * 512 + 512]
                    ins = t_.matmul(banks[u][0:65, q0:512],
                                    v_tm[:, 260 * sch + 65 * gloc:
                                         260 * sch + 65 * gloc + 65],
                                    psl, start=st, stop=sp,
                                    skip_group_check=True)
                ins.then_inc(S("sPV"), 1)
                if sp:
                    t_.wait_ge(S("sEVC"), R + 1)
                    osb = V(A_OSB, F32, 512)
                    dsb = V(A_DSB, F32, 512)
                    ins = None
                    for j in range(4):
                        t_.transpose(PVB[R % 2][:, 128 * j:128 * j + 128],
                                     osb[:, 128 * j:128 * j + 128], ident_f[:])
                        ins = t_.transpose(DNB2[R % 2][:, 128 * j:128 * j + 128],
                                           dsb[:, 128 * j:128 * j + 128],
                                           ident_f[:])
                    ins.then_inc(S("sPT2"), 1)
                    if half == 1:
                        pending_bt.append(R)
            for Rbt in pending_bt:
                t_.wait_ge(S("sSTG"), Rbt + 1)
                stg = V(A_STG, F32, 4 * 128)
                ins2 = None
                for j in range(4):
                    ins2 = t_.transpose(
                        PVB[Rbt % 2][:, 128 * j:128 * j + 128],
                        stg[:, 128 * j:128 * j + 128], ident_f[:])
                ins2.then_inc(S("sPT2"), 1)

            if dbg:
                t_.wait_ge(S("sDBG"), 48)
            t_.wait_ge(S("sEX"), NSTEP)
            t_.wait_ge(S("sDW"), 80)
            t_.wait_ge(S("sDM2"), 144)
            t_.wait_ge(S("sCC"), 2)
            t_.wait_ge(S("sTL"), 6)
            t_.wait_ge(S("sDM3"), 32)
            t_.wait_ge(S("sEPI"), NROUND)
            for tt in range(4):
                for e in range(NU):
                    i_u = tt * NU + e
                    goff = F_SLG + (i_u % 2) * 8192
                    uoff = F_SLU + (i_u % 2) * 8192
                    gsl = V(goff, BF16, 4096)
                    usl = V(uoff, BF16, 4096)
                    t_.wait_ge(S("sDFF"), 16 * SDFF_AT[("gu", tt, e)])
                    if e < NE:
                        t_.wait_ge(S("sFZ"), FFN_E_BEFORE[(tt, e)])
                        t_.matmul(
                            ps[2][:, 0:512], ones_r_bf[:],
                            m_rows[0:1, 2048 * e + 512 * tt:
                                   2048 * e + 512 * tt + 512],
                            start=True, stop=True,
                            skip_group_check=True).then_inc(S("sFM"), 1)
                    for ic in range(4):
                        cnt = FFN_IC_BEFORE[(tt, e, ic)]
                        if cnt >= 1:
                            t_.wait_ge(S("sFA"), cnt)
                        ins = None
                        for fc in range(8):
                            ins = t_.matmul(
                                ps[0][:, 0:512],
                                gsl[:, fc * 512 + 128 * ic:
                                    fc * 512 + 128 * ic + 128],
                                xn_own[:, fc * 2048 + 512 * tt:
                                       fc * 2048 + 512 * tt + 512],
                                start=(fc == 0), stop=(fc == 7),
                                skip_group_check=True)
                        ins.then_inc(S("sFG"), 1)
                        if cnt >= 1:
                            t_.wait_ge(S("sFP"), cnt)
                        ins = None
                        for fc in range(8):
                            ins = t_.matmul(
                                ps[1][:, 0:512],
                                usl[:, fc * 512 + 128 * ic:
                                    fc * 512 + 128 * ic + 128],
                                xn_own[:, fc * 2048 + 512 * tt:
                                       fc * 2048 + 512 * tt + 512],
                                start=(fc == 0), stop=(fc == 7),
                                skip_group_check=True)
                        ins.then_inc(S("sFU"), 1)
                for dc in range(8):
                    idxd = tt * 8 + dc
                    t_.wait_ge(S("sDFF"), 16 * SDFF_AT[("dn", tt, dc)])
                    t_.wait_ge(S("sFZ"), FFN_TT_END[tt])
                    if idxd >= 1:
                        t_.wait_ge(S("sFE"), idxd)
                    dsl = V(F_SLD + (idxd % 2) * 8192, BF16, 8 * 512)
                    ssl = V(F_SDS, BF16, 8 * 128)
                    wsl = V(F_WOS, BF16, 8 * 128)
                    att = V(F_ATT, BF16, 8 * 512)
                    first = True
                    for hc in range(8):
                        t_.matmul(ps[3][:, 0:512],
                                  wsl[:, 128 * hc:128 * hc + 128],
                                  att[:, 512 * hc:512 * hc + 512],
                                  start=first, stop=False, skip_group_check=True)
                        first = False
                    for e in range(NE):
                        for ic in range(4):
                            t_.matmul(ps[3][:, 0:512],
                                      dsl[:, e * 512 + 128 * ic:
                                          e * 512 + 128 * ic + 128],
                                      V(F_Z + e * 4096 + ic * 1024, BF16, 512),
                                      start=False, stop=False,
                                      skip_group_check=True)
                    ins = None
                    for ic in range(8):
                        ins = t_.matmul(ps[3][:, 0:512],
                                        ssl[:, 128 * ic:128 * ic + 128],
                                        V(F_ZSH + ic * 1024, BF16, 512),
                                        start=False, stop=(ic == 7),
                                        skip_group_check=True)
                    ins.then_inc(S("sFD"), 1)
                for j in range(4):
                    idxr = tt * 4 + j
                    t_.wait_ge(S("sFE"), 8 * (tt + 1))
                    if idxr >= 1:
                        t_.wait_ge(S("sFO"), idxr)
                    ins = None
                    for dc in range(8):
                        tgt = ps[4] if dc < 4 else ps[5]
                        ins = t_.transpose(
                            tgt[:, 128 * (dc % 4):128 * (dc % 4) + 128],
                            V(F_OFM + dc * 2048, F32, 512)
                            [:, 128 * j:128 * j + 128],
                            ident_f[:])
                    ins.then_inc(S("sFT"), 1)

    return nc


def _rope_tables():
    inv_freq = 1.0 / 10000.0 ** (np.arange(0, HD, 2, dtype=np.float32) / HD)
    tt = np.arange(T, dtype=np.float32)
    freqs = np.outer(tt, inv_freq).astype(np.float32)
    cos = np.cos(freqs).astype(np.float32)
    sin = np.sin(freqs).astype(np.float32)
    cos2 = np.concatenate([cos, cos], 1)
    sin2 = np.concatenate([sin, -sin], 1)
    return (np.ascontiguousarray(np.tile(cos2, (1, 16))),
            np.ascontiguousarray(np.tile(sin2, (1, 16))))


ROPE_TABLES = _rope_tables()


def host_prepare(inputs, core):
    b, p = divmod(core, 2)
    x = np.ascontiguousarray(np.asarray(inputs["L0_attn_x"], np.float32)[b])
    Wq = np.asarray(inputs["L0_attn_Wq"], np.float32)
    Wk = np.asarray(inputs["L0_attn_Wk"], np.float32)
    Wv = np.asarray(inputs["L0_attn_Wv"], np.float32)
    qcols = []
    for rr in range(4):
        ja, jb = RR_HEADS[rr]
        for jh in (ja, jb):
            hgl = 8 * p + jh
            for c in range(2):
                m = 2 * hgl + c
                qcols.extend(range(32 * m, 32 * m + 32))
    egw = np.asarray(inputs["L0_ffn_expert_gate_w"], np.float32)
    euw = np.asarray(inputs["L0_ffn_expert_up_w"], np.float32)
    edw = np.asarray(inputs["L0_ffn_expert_down_w"], np.float32)
    sgw = np.asarray(inputs["L0_ffn_shared_gate_w"], np.float32)
    suw = np.asarray(inputs["L0_ffn_shared_up_w"], np.float32)
    sdw = np.asarray(inputs["L0_ffn_shared_down_w"], np.float32)
    return dict(
        x4096=x,
        x_own=np.ascontiguousarray(x[2048 * p:2048 * p + 2048]),
        wq_eff=np.ascontiguousarray(Wq[:, qcols]),
        wkv_eff=np.ascontiguousarray(np.concatenate(
            [Wk[:, 256 * p:256 * p + 256], Wv[:, 256 * p:256 * p + 256]], 1)),
        wo_in=np.ascontiguousarray(np.asarray(inputs["L0_attn_Wo"], np.float32)),
        lamv=np.concatenate([
            np.asarray(inputs["L0_attn_lambda_q1"], np.float32),
            np.asarray(inputs["L0_attn_lambda_k1"], np.float32),
            np.asarray(inputs["L0_attn_lambda_q2"], np.float32),
            np.asarray(inputs["L0_attn_lambda_k2"], np.float32)]).reshape(1, 128),
        rmsw_in=np.asarray(inputs["L0_attn_rms_w"], np.float32).reshape(1, 64),
        gwt_in=np.ascontiguousarray(
            np.asarray(inputs["L0_ffn_gate_w"], np.float32).T),
        lbb=np.asarray(inputs["L0_ffn_lb_bias"], np.float32).reshape(1, 8),
        egw_t=np.ascontiguousarray(egw.transpose(0, 2, 1)),
        euw_t=np.ascontiguousarray(euw.transpose(0, 2, 1)),
        edw_t2=np.ascontiguousarray(
            edw.transpose(0, 2, 1).reshape(NE, 512, 8, 128).transpose(0, 2, 1, 3)),
        sgw_t=np.ascontiguousarray(sgw.T),
        suw_t=np.ascontiguousarray(suw.T),
        sdw_t2=np.ascontiguousarray(
            sdw.T.reshape(1024, 8, 128).transpose(1, 0, 2)),
        cosr=ROPE_TABLES[0], sinr=ROPE_TABLES[1],
    )


def kernel(**inputs):
    nc = build_nc()
    in_maps = [host_prepare(inputs, c) for c in range(8)]
    res = run_bass_kernel_spmd(nc, in_maps, core_ids=list(range(8)))
    out = np.empty((4, T, E), np.float32)
    for c in range(8):
        b, p = divmod(c, 2)
        out[b, 2048 * p:2048 * p + 2048] = res.results[c]["out_d"]
    return out


# version 19 (final)
# speedup vs baseline: 1.0130x; 1.0130x over previous
import math
from contextlib import ExitStack

import numpy as np

import concourse.bass as bass
import concourse.mybir as mybir
from concourse.bass import ds
from concourse.bass_utils import run_bass_kernel_spmd

F32 = mybir.dt.float32
BF16 = mybir.dt.bfloat16
AF = mybir.ActivationFunctionType
ALU = mybir.AluOpType

E = 1024
HD = 32
T = 4096
NT = 32
NQT = 8
NE = 8
LAMBDA_INIT = 0.8 - 0.6 * math.exp(-0.3 * 12)
B_SHIFT = 12.0
SCALE = HD ** -0.5
EPS_LN = 1e-5
EPS_RMS = 1e-6

RR_HEADS = [(0, 3), (1, 2), (4, 7), (5, 6)]

STEPS = []
for _Tq in range(NQT):
    _ns = 4 * (_Tq + 1)
    for _rr in range(4):
        for _half in range(2):
            for _s in range(_ns):
                STEPS.append((_Tq, _rr, _half, _s, _ns))
NSTEP = len(STEPS)
NROUND = NQT * 4 * 2
STEP_ROUND = []
MK_AFTER = []
ROUND_LAST = {}
_r, _key, _mk = -1, None, 0
for _k, (_Tq, _rr, _half, _s, _ns) in enumerate(STEPS):
    if (_Tq, _rr, _half) != _key:
        _key = (_Tq, _rr, _half)
        _r += 1
    STEP_ROUND.append(_r)
    ROUND_LAST[_r] = _k
    if _s >= 4 * _Tq:
        _mk += 1
    MK_AFTER.append(_mk)


def is_diag(k):
    Tq, rr, half, s, ns = STEPS[k]
    return s >= 4 * Tq


NU = NE + 2


def ffn_nic(e):
    return 4


FFN_UNITS = [(tt, e) for tt in range(4) for e in range(NU)]
FFN_IC_BEFORE = {}
IC_END = []
_c = 0
for _i, (_tt, _e) in enumerate(FFN_UNITS):
    for _ic in range(4):
        FFN_IC_BEFORE[(_tt, _e, _ic)] = _c
        _c += 1
    IC_END.append(_c)
FFN_IC_TOTAL = _c
FFN_E_BEFORE = {(tt, e): FFN_IC_BEFORE[(tt, e, 0)] for (tt, e) in FFN_UNITS}
FFN_TT_END = {tt: IC_END[tt * NU + NU - 1] for tt in range(4)}

SLOT_GUARD = [IC_END[_i - 2] if _i >= 2 else 0 for _i in range(len(FFN_UNITS))]

SDFF_AT = {}
_c = 0
for _tt in range(4):
    for _e in range(NU):
        _c += 2
        SDFF_AT[("gu", _tt, _e)] = _c
    _c += 1
    SDFF_AT[("slab", _tt)] = _c
    for _dc in range(8):
        _c += 10
        SDFF_AT[("dn", _tt, _dc)] = _c
SDFF_TOTAL = _c

PT2_BEFORE = {}
_c = 0
for _R in range(NROUND):
    PT2_BEFORE[_R] = _c
    _c += 1
    if _R % 2 == 1:
        _c += 1
PT2_TOTAL = _c

O_QFM = 0
O_KFM = 32 * 1024
O_VTM = 48 * 1024
O_SCR = 64 * 1024 + 256
O_MROW = 128 * 1024
BIGK = 160 * 1024
P_X = O_SCR
P_COS = P_X + 8192
P_SIN = P_COS + 4096
P_XNF = P_SIN + 4096
P_XNB = P_XNF + 4096
P_FMT = P_XNB + 4096
P_RT = P_FMT + 4096
P_QR = P_RT + 4096
P_KR = P_QR + 1024
P_GP = P_KR + 512
P_MBF = P_GP + 64
P_MT = P_MBF + 512
P_WQ = P_MT + 8192
P_WK = P_WQ + 8192
P_WV = P_WK + 4096
P_DUMP = P_WV + 4096
assert P_DUMP + 4096 <= O_MROW
A_OSB = O_SCR
A_DSB = A_OSB + 2048
A_AF = A_DSB + 2048
A_SS = A_AF + 1024
A_RS = A_SS + 512
A_STG = A_RS + 512
A_SQ = A_STG + 2048
assert A_SQ + 1024 <= P_MT
F_SLG = 0
F_SLU = 16 * 1024
F_SLD = 32 * 1024
F_SDS = 48 * 1024
F_WOS = 50 * 1024
F_Z = 52 * 1024
F_ZSH = 84 * 1024
F_SIG = 92 * 1024
F_PS = 94 * 1024
F_ATT = 96 * 1024
F_OFM = 104 * 1024
F_XR = 120 * 1024
F_OSB = 124 * 1024
assert F_OSB + 4096 <= O_MROW


def build_nc(dbg=False):
    nc = bass.Bass("TRN2", target_bir_lowering=False,
                   detect_race_conditions=False)

    x4096 = nc.dram_tensor("x4096", [T, E], F32, kind="ExternalInput").ap()
    x_own = nc.dram_tensor("x_own", [2048, E], F32, kind="ExternalInput").ap()
    wq_eff = nc.dram_tensor("wq_eff", [E, 512], F32, kind="ExternalInput").ap()
    wkv_eff = nc.dram_tensor("wkv_eff", [E, 512], F32,
                             kind="ExternalInput").ap()
    wo_in = nc.dram_tensor("wo_in", [E, E], F32, kind="ExternalInput").ap()
    lamv = nc.dram_tensor("lamv", [1, 128], F32, kind="ExternalInput").ap()
    rmsw_in = nc.dram_tensor("rmsw_in", [1, 64], F32, kind="ExternalInput").ap()
    gwt_in = nc.dram_tensor("gwt_in", [E, 8], F32, kind="ExternalInput").ap()
    lbb = nc.dram_tensor("lbb", [1, 8], F32, kind="ExternalInput").ap()
    egw_t = nc.dram_tensor("egw_t", [NE, E, 512], F32, kind="ExternalInput").ap()
    euw_t = nc.dram_tensor("euw_t", [NE, E, 512], F32, kind="ExternalInput").ap()
    edw_t2 = nc.dram_tensor("edw_t2", [NE, 8, 512, 128], F32,
                            kind="ExternalInput").ap()
    sgw_t = nc.dram_tensor("sgw_t", [E, E], F32, kind="ExternalInput").ap()
    suw_t = nc.dram_tensor("suw_t", [E, E], F32, kind="ExternalInput").ap()
    sdw_t2 = nc.dram_tensor("sdw_t2", [8, E, 128], F32, kind="ExternalInput").ap()
    cosr = nc.dram_tensor("cosr", [T, 512], F32, kind="ExternalInput").ap()
    sinr = nc.dram_tensor("sinr", [T, 512], F32, kind="ExternalInput").ap()
    out_d = nc.dram_tensor("out_d", [2048, E], F32, kind="ExternalOutput").ap()

    egw_b = nc.dram_tensor("egw_b", [NE, E, 512], BF16).ap()
    euw_b = nc.dram_tensor("euw_b", [NE, E, 512], BF16).ap()
    edw_b = nc.dram_tensor("edw_b", [NE, 8, 512, 128], BF16).ap()
    sgw_b = nc.dram_tensor("sgw_b", [E, E], BF16).ap()
    suw_b = nc.dram_tensor("suw_b", [E, E], BF16).ap()
    sdw_b = nc.dram_tensor("sdw_b", [8, E, 128], BF16).ap()
    wo_b = nc.dram_tensor("wo_b", [E, E], BF16).ap()
    xn_d = nc.dram_tensor("xn_d", [8, 128, T], BF16).ap()
    m_dr2 = nc.dram_tensor("m_dr2", [256, 128], BF16).ap()
    dko = dict(kind="ExternalOutput") if dbg == 2 else {}
    attn_my = nc.dram_tensor("attn_my", [512, T], BF16).ap()
    attn_full = nc.dram_tensor("attn_full", [1024, T], BF16).ap()
    cc_in = nc.dram_tensor("cc_in", [1, 16], F32).ap()
    cc_out = nc.dram_tensor("cc_out", [1, 16], F32).ap()
    if dbg:
        dbg_q = nc.dram_tensor("dbg_q", [128, 4 * 4096], BF16, **dko).ap()
        dbg_k = nc.dram_tensor("dbg_k", [128, 2 * 4096], BF16, **dko).ap()
        dbg_v = nc.dram_tensor("dbg_v", [128, 32 * 260], BF16, **dko).ap()
    if dbg == 3:
        dump_gp = nc.dram_tensor("dump_gp", [128, NT * 16], F32,
                                 kind="ExternalOutput").ap()
        dump_gs = nc.dram_tensor("dump_gs", [128, NT * 32], F32,
                                 kind="ExternalOutput").ap()
    if dbg == 2:
        dump_xn = nc.dram_tensor("dump_xn", [128, 8 * 2048], BF16,
                                 kind="ExternalOutput").ap()
        dump_mrow = nc.dram_tensor("dump_mrow", [1, 8 * 2048], BF16,
                                   kind="ExternalOutput").ap()
        dump_cons = nc.dram_tensor("dump_cons", [128, 201], F32,
                                   kind="ExternalOutput").ap()
        dump_z = nc.dram_tensor("dump_z", [128, 20480], BF16,
                                kind="ExternalOutput").ap()
        dump_ofm = nc.dram_tensor("dump_ofm", [128, 4096], F32,
                                  kind="ExternalOutput").ap()
        dump_am = nc.dram_tensor("dump_am", [512, T], BF16,
                                 kind="ExternalOutput").ap()
        dump_af = nc.dram_tensor("dump_af", [1024, T], BF16,
                                 kind="ExternalOutput").ap()

    with ExitStack() as ctx:
        en = ctx.enter_context

        big = en(nc.sbuf_tensor("big", [128, BIGK], mybir.dt.uint8))
        xn_own = en(nc.sbuf_tensor("xn_own", [128, 8 * 2048], BF16))
        p_sb = en(nc.sbuf_tensor("p_sb", [128, 4 * 512], BF16))
        slab = en(nc.sbuf_tensor("slab", [128, 4 * 512], BF16))
        ident_bf = en(nc.sbuf_tensor("ident_bf", [128, 128], BF16))
        ident_f = en(nc.sbuf_tensor("ident_f", [128, 128], F32))
        ones_c_bf = en(nc.sbuf_tensor("ones_c_bf", [128, 1], BF16))
        ones64_bf = en(nc.sbuf_tensor("ones64_bf", [128, 64], BF16))
        ones_c_f = en(nc.sbuf_tensor("ones_c_f", [128, 1], F32))
        ones_r_bf = en(nc.sbuf_tensor("ones_r_bf", [1, 128], BF16))
        ones_r_f = en(nc.sbuf_tensor("ones_r_f", [1, 128], F32))
        lam_col = en(nc.sbuf_tensor("lam_col", [128, 1], F32))
        rmsw_rep = en(nc.sbuf_tensor("rmsw_rep", [128, 64], F32))
        wm_cols = en(nc.sbuf_tensor("wm_cols", [128, 8], F32))
        m_diag = en(nc.sbuf_tensor("m_diag", [128, 4 * 512], BF16))
        gsc = en(nc.sbuf_tensor("gsc", [128, 128], F32))
        lam_sc = en(nc.sbuf_tensor("lam_sc", [1, 128], F32))
        gwt_f = en(nc.sbuf_tensor("gwt_f", [128, 64], F32))
        gwT = en(nc.sbuf_tensor("gwT", [128, 128], BF16))
        lb_row = en(nc.sbuf_tensor("lb_row", [1, 8], F32))
        wmean_r = en(nc.sbuf_tensor("wmean_r", [1, 16], F32))
        c_negB = en(nc.sbuf_tensor("c_negB", [128, 1], F32))
        c_one = en(nc.sbuf_tensor("c_one", [128, 1], F32))
        c_eps5 = en(nc.sbuf_tensor("c_eps5", [128, 1], F32))
        c_eps6 = en(nc.sbuf_tensor("c_eps6", [128, 1], F32))
        nc.const_aps.register(F32, -float(B_SHIFT), c_negB.ap())
        nc.const_aps.register(F32, 1.0, c_one.ap())
        nc.const_aps.register(F32, float(EPS_LN), c_eps5.ap())
        nc.const_aps.register(F32, float(EPS_RMS), c_eps6.ap())

        def V(off, dt, cols, p0=0, np_=128):
            return big[p0:p0 + np_, off:off + cols * mybir.dt.size(dt)].bitcast(dt)

        q_fm = V(O_QFM, BF16, 4 * 4096)
        k_fm = V(O_KFM, BF16, 2 * 4096)
        v_tm = V(O_VTM, BF16, 32 * 260)
        assert O_VTM + 32 * 260 * 2 <= O_SCR
        v_tm4 = v_tm.rearrange("p (c g o) -> p c g o", c=32, g=4)
        m_rows = V(O_MROW, BF16, 8 * 2048, 0, 1)

        sems = {}
        for name in ["sDX", "sDT", "sDW", "sDC", "sDXN", "sDSL", "sDFF", "sDXR",
                     "sDO", "sDM", "sDM2", "sGP", "sPR", "sSQ", "sRST", "sXN",
                     "sXEV", "sPXT", "sPQ", "sRP", "sPT", "sQKE", "sGEX", "sCC",
                     "sDM3", "sTL", "sSC", "sEX", "sMK", "sPV", "sEVC", "sPT2", "sSS", "sST", "sRO",
                     "sRS", "sSTG", "sEPI", "sFG", "sFU", "sFA", "sFZ", "sFM",
                     "sFD", "sFE", "sFT", "sFO", "sDBG", "sDGP", "sVV",
                     "sFP", "sPP"]:
            sems[name] = en(nc.semaphore(name))

        def S(n):
            return sems[n]

        block = en(nc.Block())

        psq = [en(nc.psum_tensor(f"psq{i}", [128, 1024], F32))
               for i in range(2)]
        ps = [psq[0][:, 0:512], psq[0][:, 512:1024],
              psq[1][:, 0:512], psq[1][:, 512:1024]]
        ps += [en(nc.psum_tensor(f"ps{i}", [128, 512], F32))
               for i in range(4, 7)]
        psB = en(nc.psum_tensor("psB", [128, 1024], BF16))
        psBf = psB[:, 0:1024].bitcast(F32)
        ps6bf = ps[6][:, 0:512].bitcast(BF16)
        SCB = {(0, 0): ps[0], (0, 1): ps[1], (1, 0): ps[2], (1, 1): ps[3]}
        PVB = {0: ps[4], 1: ps[5]}
        DNB2 = {0: ps[6], 1: psBf}

        @block.sync
        def _(sy):
            pid = sy.partition_id()
            parity = sy.scalar_reg_alu(ALU.mod, pid, 2)
            tok0 = sy.scalar_reg_alu(ALU.mult, parity, 2048)

            sy.dma_start(out=lam_sc[:], in_=lamv).then_inc(S("sDW"), 16)
            sy.dma_start(out=gsc[0:1, 48:112], in_=rmsw_in).then_inc(S("sDW"), 16)
            sy.dma_start(out=gwt_f[:].rearrange("p (f e) -> p f e", f=8),
                         in_=gwt_in.rearrange("(f p) e -> p f e", p=128)
                         ).then_inc(S("sDW"), 16)
            sy.dma_start(out=lb_row[:], in_=lbb).then_inc(S("sDW"), 16)

            for t in range(NT):
                if t >= 2:
                    sy.wait_ge(S("sXN"), 2 * (t - 1))
                    sy.wait_ge(S("sSQ"), t - 1)
                    sy.wait_ge(S("sRP"), t - 1)
                sy.dma_start(out=V(P_X + (t % 2) * 4096, F32, 1024),
                             in_=x4096[128 * t:128 * t + 128, :]
                             ).then_inc(S("sDX"), 16)
                sy.dma_start(out=V(P_COS + (t % 2) * 2048, F32, 512),
                             in_=cosr[128 * t:128 * t + 128, :]
                             ).then_inc(S("sDT"), 16)
                sy.dma_start(out=V(P_SIN + (t % 2) * 2048, F32, 512),
                             in_=sinr[128 * t:128 * t + 128, :]
                             ).then_inc(S("sDT"), 16)
                if dbg == 3 and t >= 1:
                    sy.wait_ge(S("sRP"), t)
                    sy.dma_start(out=dump_gp[:, 16 * (t - 1):16 * (t - 1) + 16],
                                 in_=V(P_GP, F32, 16)).then_inc(S("sDGP"), 16)
                    sy.dma_start(out=dump_gs[:, 32 * (t - 1):32 * (t - 1) + 32],
                                 in_=gsc[:, 16:48]).then_inc(S("sDGP"), 16)
            if dbg == 3:
                sy.wait_ge(S("sRP"), NT)
                sy.dma_start(out=dump_gp[:, 16 * (NT - 1):16 * NT],
                             in_=V(P_GP, F32, 16)).then_inc(S("sDGP"), 16)
                sy.dma_start(out=dump_gs[:, 32 * (NT - 1):32 * NT],
                             in_=gsc[:, 16:48]).then_inc(S("sDGP"), 16)

            sy.wait_ge(S("sDXN"), 16 * NT)
            sy.dma_start(out=xn_own[:].rearrange("p (f t) -> p f t", f=8),
                         in_=xn_d[:, :, ds(tok0, 2048)]
                             .rearrange("f p t -> p f t")).then_inc(S("sDW"), 16)

            for Tq in range(NQT):
                last_round = (Tq * 4 + 3) * 2 + 1
                sy.wait_ge(S("sEPI"), last_round + 1)
                sy.dma_start(
                    out=attn_my[:, 512 * Tq:512 * Tq + 512]
                        .rearrange("(c p) t -> p c t", p=128),
                    in_=slab[:].rearrange("p (c t) -> p c t", c=4)
                ).then_inc(S("sDSL"), 16)

            sy.wait_ge(S("sDM2"), 16)
            q16 = sy.scalar_reg_alu(ALU.mult, parity, 16)
            m2v = m_dr2.rearrange("(jq e) p -> e jq p", e=8)
            for e in range(8):
                sy.dma_start(out=m_rows[0:1, 2048 * e:2048 * e + 2048]
                             .rearrange("o (q p) -> o q p", q=16),
                             in_=m2v[e:e + 1, ds(q16, 16), :]
                             ).then_inc(S("sDM2"), 16)
            if dbg:
                sy.wait_ge(S("sCC"), 2)
                sy.dma_start(out=dbg_q, in_=q_fm).then_inc(S("sDBG"), 16)
                sy.dma_start(out=dbg_k, in_=k_fm).then_inc(S("sDBG"), 16)
                sy.dma_start(out=dbg_v, in_=v_tm).then_inc(S("sDBG"), 16)
            sy.wait_ge(S("sDC"), 16 * 9)
            sy.wait_ge(S("sDM2"), 16)
            sy.wait_ge(S("sSC"), NSTEP)
            for tt in range(4):
                for e in range(NU):
                    i_u = tt * NU + e
                    if SLOT_GUARD[i_u] > 0:
                        sy.wait_ge(S("sFG"), SLOT_GUARD[i_u])
                    par = i_u % 2
                    if e < NE:
                        gsrc = egw_b[e].rearrange("(f p) o -> p f o", p=128)
                        usrc = euw_b[e].rearrange("(f p) o -> p f o", p=128)
                    else:
                        hh = e - NE
                        gsrc = sgw_b.rearrange("(f p) o -> p f o", p=128)[
                            :, :, 512 * hh:512 * hh + 512]
                        usrc = suw_b.rearrange("(f p) o -> p f o", p=128)[
                            :, :, 512 * hh:512 * hh + 512]
                    sy.dma_start(
                        out=V(F_SLG + par * 8192, BF16, 4096)
                            .rearrange("p (f o) -> p f o", f=8),
                        in_=gsrc).then_inc(S("sDFF"), 16)
                    sy.dma_start(
                        out=V(F_SLU + par * 8192, BF16, 4096)
                            .rearrange("p (f o) -> p f o", f=8),
                        in_=usrc).then_inc(S("sDFF"), 16)
                if tt == 0:
                    sy.wait_ge(S("sCC"), 2)
                    sy.wait_ge(S("sPV"), NSTEP)
                else:
                    sy.wait_ge(S("sFD"), 8 * tt)
                sy.dma_start(
                    out=V(F_ATT, BF16, 8 * 512).rearrange("p (c t) -> p c t", c=8),
                    in_=attn_full.rearrange("(c p) t -> p c t", p=128)
                        [:, :, ds(tok0 + 512 * tt, 512)]).then_inc(S("sDFF"), 16)
                for dc in range(8):
                    idxd = tt * 8 + dc
                    if idxd >= 2:
                        sy.wait_ge(S("sFD"), idxd - 1)
                    par = idxd % 2
                    for e in range(NE):
                        sy.dma_start(
                            out=V(F_SLD + par * 8192 + e * 1024, BF16, 512)
                                .rearrange("p (i o) -> p i o", i=4),
                            in_=edw_b[e, dc].rearrange("(i p) o -> p i o", p=128)
                        ).then_inc(S("sDFF"), 16)
                    if idxd >= 1:
                        sy.wait_ge(S("sFD"), idxd)
                    sy.dma_start(
                        out=V(F_SDS, BF16, 8 * 128)
                            .rearrange("p (i o) -> p i o", i=8),
                        in_=sdw_b[dc].rearrange("(i p) o -> p i o", p=128)
                    ).then_inc(S("sDFF"), 16)
                    sy.dma_start(
                        out=V(F_WOS, BF16, 8 * 128)
                            .rearrange("p (h o) -> p h o", h=8),
                        in_=wo_b.rearrange("(h p) o -> p h o", p=128)
                            [:, :, 128 * dc:128 * dc + 128]).then_inc(S("sDFF"), 16)
                for j in range(4):
                    idxr = tt * 4 + j
                    if idxr >= 1:
                        sy.wait_ge(S("sFO"), idxr)
                    sy.dma_start(
                        out=V(F_XR, F32, 1024),
                        in_=x_own[512 * tt + 128 * j:512 * tt + 128 * j + 128, :]
                    ).then_inc(S("sDXR"), 16)
                    sy.wait_ge(S("sFO"), idxr + 1)
                    sy.dma_start(
                        out=out_d[512 * tt + 128 * j:512 * tt + 128 * j + 128, :],
                        in_=V(F_OSB, F32, 1024)
                    ).then_inc(S("sDO"), 16)

            sy.wait_ge(S("sDO"), 16 * 16)
            if dbg == 2:
                sy.dma_start(out=dump_xn, in_=xn_own[:]).then_inc(S("sDBG"), 16)
                sy.dma_start(out=dump_mrow,
                             in_=m_rows[0:1, :]).then_inc(S("sDBG"), 16)
                with nc.allow_non_contiguous_dma(reason="debug dumps"):
                    sy.dma_start(out=dump_cons[:, 0:1],
                                 in_=lam_col[:]).then_inc(S("sDBG"), 16)
                    sy.dma_start(out=dump_cons[:, 1:65],
                                 in_=rmsw_rep[:]).then_inc(S("sDBG"), 16)
                    sy.dma_start(out=dump_cons[:, 65:73],
                                 in_=wm_cols[:]).then_inc(S("sDBG"), 16)
                    sy.dma_start(out=dump_cons[:, 73:201],
                                 in_=gsc[:]).then_inc(S("sDBG"), 16)
                sy.dma_start(out=dump_z,
                             in_=V(F_Z, BF16, 20480)).then_inc(S("sDBG"), 16)
                sy.dma_start(out=dump_ofm,
                             in_=V(F_OFM, F32, 4096)).then_inc(S("sDBG"), 16)
                sy.dma_start(out=dump_am, in_=attn_my).then_inc(S("sDBG"), 16)
                sy.dma_start(out=dump_af, in_=attn_full).then_inc(S("sDBG"), 16)
                sy.wait_ge(S("sDBG"), 48 + 16 * 10)

        @block.gpsimd
        def _(g):
            pid = g.partition_id()
            parity = g.scalar_reg_alu(ALU.mod, pid, 2)
            tok0g = g.scalar_reg_alu(ALU.mult, parity, 2048)

            g.wait_ge(S("sPR"), 1)
            g.affine_select(ident_bf[:], ident_bf[:], [[1, 128]], ALU.is_equal,
                            0.0, base=0, channel_multiplier=-1
                            ).then_inc(S("sGP"), 1)
            g.affine_select(ident_f[:], ident_f[:], [[1, 128]], ALU.is_equal,
                            0.0, base=0, channel_multiplier=-1
                            ).then_inc(S("sGP"), 1)
            for dd in range(4):
                g.affine_select(
                    m_diag[:, 512 * dd:512 * dd + 512],
                    m_diag[:, 512 * dd:512 * dd + 512],
                    [[1, 512]], ALU.is_ge, 0.0,
                    base=-128 * dd, channel_multiplier=-1
                ).then_inc(S("sGP"), 1)

            g.dma_start(out=V(P_WQ, BF16, 8 * 512)
                        .rearrange("p (f n) -> p f n", f=8),
                        in_=wq_eff.rearrange("(f p) n -> p f n", p=128)
                        ).then_inc(S("sDC"), 16)
            g.dma_start(out=V(P_WK, BF16, 8 * 512)
                        .rearrange("p (f n) -> p f n", f=8),
                        in_=wkv_eff.rearrange("(f p) n -> p f n", p=128)
                        ).then_inc(S("sDC"), 16)
            g.wait_ge(S("sRP"), 8)
            for dst, src in [(egw_b, egw_t), (euw_b, euw_t), (edw_b, edw_t2),
                             (sgw_b, sgw_t), (suw_b, suw_t), (sdw_b, sdw_t2),
                             (wo_b, wo_in)]:
                g.dma_start(out=dst, in_=src).then_inc(S("sDC"), 16)

            g.wait_ge(S("sTL"), 2)
            g.dma_start(out=cc_in, in_=gsc[0:1, 112:128]).then_inc(S("sDM3"), 16)
            g.wait_ge(S("sDM3"), 16)
            g.collective_compute(
                "AllReduce", ALU.add, replica_groups=[list(range(8))],
                ins=[cc_in], outs=[cc_out]).then_inc(S("sCC"), 1)
            g.wait_ge(S("sCC"), 1)
            g.dma_start(out=wmean_r[:], in_=cc_out).then_inc(S("sDM3"), 16)

            g.wait_ge(S("sDSL"), 16 * NQT)
            g.collective_compute(
                "AllGather", ALU.bypass,
                replica_groups=[[0, 1], [2, 3], [4, 5], [6, 7]],
                ins=[attn_my], outs=[attn_full]).then_inc(S("sCC"), 1)

        @block.scalar
        def _(s):
            s.wait_ge(S("sPR"), 1)
            for t in range(NT):
                s.wait_ge(S("sDX"), 16 * (t + 1))
                if t >= 1:
                    s.wait_ge(S("sXN"), 2 * t - 1)
                xv = V(P_X + (t % 2) * 4096, F32, 1024)
                s.activation(V(P_DUMP, F32, 1024), xv, AF.Square,
                             accum_out=gsc[:, 1:2]).then_inc(S("sSQ"), 1)
                s.wait_ge(S("sXN"), 2 * t + 1)
                s.activation(gsc[:, 4:5], gsc[:, 3:4], AF.Ln, bias=float(EPS_LN))
                s.activation(gsc[:, 5:6], gsc[:, 4:5], AF.Exp,
                             scale=-0.5).then_inc(S("sRST"), 1)
                s.wait_ge(S("sPQ"), t + 1)
                if t >= 1:
                    s.wait_ge(S("sRP"), t)
                s.activation(gsc[:, 8:16], ps[6][:, 8 * (t % 2):8 * (t % 2) + 8],
                             AF.Exp, scale=-1.0).then_inc(S("sGEX"), 1)
                s.wait_ge(S("sXEV"), t + 1)
                s.dma_start(
                    out=xn_d[:, :, 128 * t:128 * t + 128]
                        .rearrange("f p t -> p f t"),
                    in_=V(P_FMT + (t % 2) * 2048, BF16, 1024)
                        .rearrange("p (f t) -> p f t", f=8)).then_inc(S("sDXN"), 16)

            s.wait_ge(S("sTL"), 3)
            s.activation(gsc[0:1, 11:12], gsc[0:1, 9:10], AF.Exp)
            s.activation(gsc[0:1, 12:13], gsc[0:1, 10:11],
                         AF.Exp).then_inc(S("sRST"), 1)
            s.wait_ge(S("sDM"), 2)
            s.dma_start(out=m_dr2.rearrange("(j q) p -> q j p", j=2),
                        in_=V(P_MT, BF16, 256)
                        .rearrange("q (j p) -> q j p", j=2)
                        ).then_inc(S("sDM2"), 16)

            pending = []
            for k in range(NSTEP):
                Tq, rr, half, sch, ns = STEPS[k]
                s.wait_ge(S("sSC"), k + 1)
                if k >= 2:
                    s.wait_ge(S("sPV"), k - 1)
                par = k % 2
                d = sch - 4 * Tq
                if d <= 0:
                    ins = s.activation(
                        p_sb[:, par * 1024:par * 1024 + 1024],
                        psq[par][:, 0:1024],
                        AF.Exp, bias=-float(B_SHIFT), scale=float(SCALE))
                else:
                    for u in range(2):
                        c0 = (par * 2 + u) * 512
                        ins = s.activation(
                            p_sb[:, c0 + 128 * d:c0 + 512],
                            psq[par][:, 512 * u + 128 * d:512 * u + 512],
                            AF.Exp, bias=-float(B_SHIFT), scale=float(SCALE))
                ins.then_inc(S("sEX"), 1)
                if sch == ns - 1:
                    pending.append(STEP_ROUND[k])
                flush = [R for R in pending if ROUND_LAST[R] <= k - 1]
                if k == NSTEP - 1:
                    flush = pending[:]
                for R in flush:
                    pending.remove(R)
                    s.wait_ge(S("sSS"), R + 1)
                    s.activation(V(A_RS, F32, 4), V(A_SS, F32, 4), AF.Ln,
                                 bias=float(EPS_RMS), scale=1.0 / 64.0)
                    s.activation(V(A_RS, F32, 4), V(A_RS, F32, 4), AF.Exp,
                                 scale=-0.5).then_inc(S("sRS"), 1)

            for tt in range(4):
                for e in range(NU):
                    for ic in range(4):
                        cnt = FFN_IC_BEFORE[(tt, e, ic)]
                        s.wait_ge(S("sFG"), cnt + 1)
                        if cnt >= 1:
                            s.wait_ge(S("sFP"), cnt)
                        sig = V(F_SIG, F32, 512)
                        s.activation(sig, ps[0][:, 0:512],
                                     AF.Silu).then_inc(S("sFA"), 1)

        @block.vector
        def _(v):
            v.memset(ones_c_bf[:], 1.0)
            v.memset(ones64_bf[:], 1.0)
            v.memset(ones_c_f[:], 1.0)
            v.memset(ones_r_bf[:], 1.0)
            v.memset(ones_r_f[:], 1.0)
            v.memset(c_negB[:], -float(B_SHIFT))
            v.memset(c_one[:], 1.0)
            v.memset(c_eps5[:], float(EPS_LN))
            v.memset(ident_bf[:], 1.0)
            v.memset(ident_f[:], 1.0)
            v.memset(m_diag[:], 1.0)
            v.memset(v_tm4[:, :, :, 64:65], 1.0)
            v.memset(c_eps6[:], float(EPS_RMS)).then_inc(S("sPR"), 1)

            vvc = [0]

            def VI(ins):
                vvc[0] += 1
                ins.then_inc(S("sVV"), 1)

            def VW():
                v.wait_ge(S("sVV"), vvc[0])

            v.wait_ge(S("sDW"), 64)
            glo = V(P_RT, F32, 64)
            VI(v.tensor_copy(gwT[:, 0:64], gwt_f[:, 0:64]))
            VW()
            VI(v.tensor_copy(glo, gwT[:, 0:64]))
            VW()
            VI(v.tensor_tensor(glo, gwt_f[:, 0:64], glo, ALU.subtract))
            VW()
            v.tensor_copy(gwT[:, 64:128], glo).then_inc(S("sTL"), 1)

            for t in range(NT):
                s_x = V(P_X + (t % 2) * 4096, F32, 1024)
                v.wait_ge(S("sDX"), 16 * (t + 1))
                VI(v.reduce_sum(gsc[:, 0:1], s_x, mybir.AxisListType.X))
                VW()
                VI(v.scalar_tensor_tensor(gsc[:, 6:7], gsc[:, 0:1],
                                          1.0 / (1024.0 * 1024.0), gsc[:, 0:1],
                                          ALU.mult, ALU.mult))
                VI(v.tensor_scalar(gsc[:, 2:3], gsc[:, 0:1], 1.0 / 1024.0,
                                   None, ALU.mult))
                v.wait_ge(S("sSQ"), t + 1)
                VW()
                v.scalar_tensor_tensor(gsc[:, 3:4], gsc[:, 1:2], 1.0 / 1024.0,
                                       gsc[:, 6:7], ALU.mult,
                                       ALU.subtract).then_inc(S("sXN"), 1)
                v.wait_ge(S("sRST"), t + 1)
                if t >= 2:
                    v.wait_ge(S("sPXT"), t - 1)
                v.tensor_scalar(V(P_XNB + (t % 2) * 2048, BF16, 1024),
                                s_x, gsc[:, 2:3], gsc[:, 5:6],
                                ALU.subtract, ALU.mult).then_inc(S("sXN"), 1)
                v.wait_ge(S("sPXT"), t + 1)
                if t >= 2:
                    v.wait_ge(S("sPQ"), t - 1)
                    v.wait_ge(S("sDXN"), 16 * (t - 1))
                v.tensor_copy(V(P_FMT + (t % 2) * 2048, BF16, 1024),
                              psB[:, 0:1024]).then_inc(S("sXEV"), 1)
                v.wait_ge(S("sPQ"), t + 1)
                v.wait_ge(S("sDT"), 16 * (2 * t + 2))
                if t >= 1:
                    v.wait_ge(S("sPT"), t)
                cosv = V(P_COS + (t % 2) * 2048, F32, 512)
                sinv = V(P_SIN + (t % 2) * 2048, F32, 512)
                rt0 = V(P_RT, F32, 512)
                rt1 = V(P_RT + 2048, F32, 512)
                pqkv = psq[t % 2]
                q3 = pqkv[:, 0:512].rearrange("p (c i) -> p c i", c=16)
                s3 = sinv.rearrange("p (c i) -> p c i", c=16)
                r13 = rt1.rearrange("p (c i) -> p c i", c=16)
                VI(v.tensor_tensor(rt0, pqkv[:, 0:512], cosv, ALU.mult))
                VI(v.tensor_tensor(r13[:, :, 0:16], q3[:, :, 16:32],
                                   s3[:, :, 0:16], ALU.mult))
                VI(v.tensor_tensor(r13[:, :, 16:32], q3[:, :, 0:16],
                                   s3[:, :, 16:32], ALU.mult))
                v.tensor_copy(v_tm4[:, t, :, 0:64],
                              pqkv[:, 768:1024].rearrange("p (g o) -> p g o",
                                                          g=4))
                VW()
                VI(v.tensor_tensor(V(P_QR, BF16, 512), rt0, rt1, ALU.add))
                k3 = pqkv[:, 512:768].rearrange("p (c i) -> p c i", c=8)
                VW()
                VI(v.tensor_tensor(rt0[:, 0:256], pqkv[:, 512:768],
                                   cosv[:, 0:256], ALU.mult))
                VI(v.tensor_tensor(r13[:, 0:8, 0:16], k3[:, :, 16:32],
                                   s3[:, 0:8, 0:16], ALU.mult))
                VI(v.tensor_tensor(r13[:, 0:8, 16:32], k3[:, :, 0:16],
                                   s3[:, 0:8, 16:32], ALU.mult))
                v.wait_ge(S("sGEX"), t + 1)
                if dbg == 3 and t >= 1:
                    v.wait_ge(S("sDGP"), 32 * t)
                e8 = gsc[:, 8:16]
                gp = V(P_GP, F32, 16)
                qe = [q_fm[:, b * 4096 + 128 * t:b * 4096 + 128 * t + 128]
                      for b in range(4)]
                ke = [k_fm[:, b * 4096 + 128 * t:b * 4096 + 128 * t + 128]
                      for b in range(2)]
                VW()
                v.tensor_tensor(V(P_KR, BF16, 256), rt0[:, 0:256],
                                rt1[:, 0:256], ALU.add).then_inc(S("sRO"), 1)
                VI(v.tensor_scalar(gsc[:, 40:48], e8, 1.0, None, ALU.add))
                v.wait_ge(S("sPT"), t + 1)
                VW()
                VI(v.reciprocal(e8, gsc[:, 40:48]))
                v.tensor_copy(qe[0], ps6bf[:, 32:160])
                VW()
                VI(v.reduce_max(gsc[:, 32:33], e8, mybir.AxisListType.X))
                v.tensor_copy(qe[1], ps6bf[:, 160:288])
                VW()
                VI(v.tensor_scalar(gsc[:, 16:24], e8, gsc[:, 32:33], None,
                                   ALU.is_ge))
                v.tensor_copy(qe[2], ps6bf[:, 288:416])
                VW()
                VI(v.reduce_sum(gsc[:, 33:34], gsc[:, 16:24],
                                mybir.AxisListType.X))
                VI(v.scalar_tensor_tensor(gsc[:, 24:32], gsc[:, 16:24],
                                          -1e30, e8, ALU.mult, ALU.add))
                v.tensor_copy(qe[3], ps6bf[:, 416:544])
                VW()
                VI(v.reduce_max(gsc[:, 34:35], gsc[:, 24:32],
                                mybir.AxisListType.X))
                VI(v.tensor_scalar(gsc[:, 35:36], gsc[:, 33:34], 1.5, None,
                                   ALU.is_ge))
                v.tensor_copy(ke[0], ps6bf[:, 544:672])
                VW()
                VI(v.tensor_tensor(gsc[:, 36:37], gsc[:, 32:33],
                                   gsc[:, 34:35], ALU.subtract))
                VW()
                VI(v.scalar_tensor_tensor(gsc[:, 37:38], gsc[:, 35:36],
                                          gsc[:, 36:37], gsc[:, 34:35],
                                          ALU.mult, ALU.add))
                v.tensor_copy(ke[1], ps6bf[:, 672:800]).then_inc(S("sQKE"), 1)
                VW()
                VI(v.tensor_scalar(gp[:, 0:8], e8, gsc[:, 37:38], None,
                                   ALU.is_ge))
                VW()
                VI(v.tensor_tensor(gsc[:, 24:32], e8, gp[:, 0:8], ALU.mult))
                v.tensor_copy(V(P_MBF, BF16, NT * 8)[:, 8 * t:8 * t + 8],
                              gp[:, 0:8])
                VW()
                VI(v.reduce_sum(gsc[:, 38:39], gsc[:, 24:32],
                                mybir.AxisListType.X))
                VW()
                VI(v.tensor_scalar(gsc[:, 38:39], gsc[:, 38:39], 1e-8,
                                   None, ALU.add))
                VW()
                VI(v.reciprocal(gsc[:, 39:40], gsc[:, 38:39]))
                VW()
                v.tensor_scalar(gp[:, 8:16], gsc[:, 24:32], gsc[:, 39:40],
                                None, ALU.mult).then_inc(S("sRP"), 1)

            v.wait_ge(S("sST"), NT)
            v.tensor_copy(gsc[0:1, 112:128],
                          ps[4][0:1, 0:16]).then_inc(S("sTL"), 1)
            v.wait_ge(S("sDW"), 16)
            if dbg == 3:
                v.wait_ge(S("sDGP"), 32 * NT)
            VI(v.tensor_tensor(gsc[0:1, 16:48], lam_sc[0:1, 0:32],
                               lam_sc[0:1, 32:64], ALU.mult))
            VW()
            VI(v.reduce_sum(gsc[0:1, 9:10], gsc[0:1, 16:48],
                            mybir.AxisListType.X))
            VW()
            VI(v.tensor_tensor(gsc[0:1, 16:48], lam_sc[0:1, 64:96],
                               lam_sc[0:1, 96:128], ALU.mult))
            VW()
            v.reduce_sum(gsc[0:1, 10:11], gsc[0:1, 16:48],
                         mybir.AxisListType.X).then_inc(S("sTL"), 1)
            v.wait_ge(S("sRST"), NT + 1)
            VI(v.scalar_tensor_tensor(gsc[0:1, 14:15], gsc[0:1, 11:12],
                                      float(LAMBDA_INIT), gsc[0:1, 12:13],
                                      ALU.add, ALU.subtract))
            v.tensor_scalar(gsc[0:1, 48:112], gsc[0:1, 48:112],
                            float(1.0 - LAMBDA_INIT), None, ALU.mult)
            VW()
            v.tensor_scalar(gsc[0:1, 15:16], gsc[0:1, 14:15], -1.0, None,
                            ALU.mult).then_inc(S("sTL"), 1)
            v.wait_ge(S("sDM3"), 32)
            VI(v.tensor_scalar(gsc[0:1, 40:48], wmean_r[0:1, 0:8], 1.0, None,
                               ALU.max))
            VW()
            VI(v.reciprocal(gsc[0:1, 0:8], gsc[0:1, 40:48]))
            VW()
            v.tensor_tensor(gsc[0:1, 32:40], wmean_r[0:1, 8:16],
                            gsc[0:1, 0:8], ALU.mult).then_inc(S("sTL"), 1)
            v.wait_ge(S("sPT"), NT + 1)
            v.tensor_copy(lam_col[:], ps[5][:, 0:1])
            v.tensor_copy(rmsw_rep[:], ps[5][:, 64:128])
            v.tensor_copy(wm_cols[:], ps[5][:, 128:136]).then_inc(S("sTL"), 1)

            v.wait_ge(S("sGP"), 6)
            v.wait_ge(S("sDM"), 1)
            v.tensor_copy(V(P_MT, BF16, 256), psB[:, 0:256]).then_inc(S("sDM"), 1)
            for k in range(NSTEP):
                Tq, rr, half, sch, ns = STEPS[k]
                par = k % 2
                R = STEP_ROUND[k]
                if is_diag(k):
                    v.wait_ge(S("sEX"), k + 1)
                    dd = sch - 4 * Tq
                    ins = None
                    for u in range(2):
                        c0 = (par * 2 + u) * 512 + 128 * dd
                        sl = p_sb[:, c0:(par * 2 + u) * 512 + 512]
                        ins = v.tensor_tensor(
                            sl, sl,
                            m_diag[:, 512 * dd + 128 * dd:512 * dd + 512],
                            ALU.mult)
                    ins.then_inc(S("sMK"), 1)
                if sch == ns - 1:
                    v.wait_ge(S("sPV"), k + 1)
                    osb = V(A_OSB, F32, 512)
                    dsb = V(A_DSB, F32, 512)
                    v.tensor_copy(V(A_OSB, F32, 512, 0, 65),
                                  PVB[R % 2][0:65, 0:512])
                    v.tensor_copy(V(A_DSB, F32, 512, 0, 65),
                                  DNB2[R % 2][0:65, 0:512]).then_inc(S("sEVC"), 1)
                    v.wait_ge(S("sPT2"), PT2_BEFORE[R] + 1)
                    af = V(A_AF, F32, 4 * 64)
                    ss = V(A_SS, F32, 4)
                    sqv = V(A_SQ, F32, 4 * 64)
                    for j in range(4):
                        v.reciprocal(gsc[:, 96 + 2 * j:97 + 2 * j],
                                     PVB[R % 2][:, 128 * j + 64:128 * j + 65])
                    for j in range(4):
                        v.reciprocal(gsc[:, 97 + 2 * j:98 + 2 * j],
                                     DNB2[R % 2][:, 128 * j + 64:128 * j + 65])
                    for j in range(4):
                        v.tensor_scalar(gsc[:, 104 + j:105 + j],
                                        gsc[:, 97 + 2 * j:98 + 2 * j],
                                        lam_col[:, 0:1], None, ALU.mult)
                    for j in range(4):
                        oT = PVB[R % 2][:, 128 * j:128 * j + 128]
                        v.tensor_scalar(af[:, 64 * j:64 * j + 64], oT[:, 0:64],
                                        gsc[:, 96 + 2 * j:97 + 2 * j], None,
                                        ALU.mult)
                    for j in range(4):
                        oT2 = DNB2[R % 2][:, 128 * j:128 * j + 128]
                        v.scalar_tensor_tensor(
                            af[:, 64 * j:64 * j + 64], oT2[:, 0:64],
                            gsc[:, 104 + j:105 + j], af[:, 64 * j:64 * j + 64],
                            ALU.mult, ALU.add)
                    for j in range(4):
                        v.tensor_tensor(sqv[:, 64 * j:64 * j + 64],
                                        af[:, 64 * j:64 * j + 64],
                                        af[:, 64 * j:64 * j + 64], ALU.mult)
                    ins = None
                    for j in range(4):
                        ins = v.reduce_sum(ss[:, j:j + 1],
                                           sqv[:, 64 * j:64 * j + 64],
                                           mybir.AxisListType.X)
                    VI(ins)
                    VW()
                    v.tensor_copy(gsc[:, 80:81], ss[:, 0:1]).then_inc(S("sSS"), 1)
                    v.wait_ge(S("sRS"), R + 1)
                    rs = V(A_RS, F32, 4)
                    stg = V(A_STG, F32, 4 * 128)
                    ja, jb = RR_HEADS[rr]
                    j_head = ja if half == 0 else jb
                    colh = 0 if (j_head % 2 == 0) else 64
                    ins = None
                    for j in range(4):
                        v.tensor_scalar(af[:, 64 * j:64 * j + 64],
                                        af[:, 64 * j:64 * j + 64],
                                        rs[:, j:j + 1], None, ALU.mult)
                    for j in range(4):
                        ins = v.tensor_tensor(
                            stg[:, 128 * j + colh:128 * j + colh + 64],
                            af[:, 64 * j:64 * j + 64], rmsw_rep[:, 0:64],
                            ALU.mult)
                    if half == 0:
                        ins.then_inc(S("sSTG"), 1)
                        v.tensor_copy(gsc[:, 80:81],
                                      rs[:, 0:1]).then_inc(S("sEPI"), 1)
                    else:
                        ins.then_inc(S("sSTG"), 1)
                        v.wait_ge(S("sPT2"), PT2_BEFORE[R] + 2)
                        if Tq >= 1:
                            v.wait_ge(S("sDSL"), 16 * Tq)
                        ch_even = (ja if ja % 2 == 0 else jb) // 2
                        ch_odd = (ja if ja % 2 == 1 else jb) // 2
                        bk = PVB[R % 2]
                        ins = None
                        for j in range(4):
                            v.tensor_copy(
                                slab[0:64, ch_even * 512 + 128 * j:
                                     ch_even * 512 + 128 * j + 128],
                                bk[0:64, 128 * j:128 * j + 128])
                            ins = v.tensor_copy(
                                slab[64:128, ch_odd * 512 + 128 * j:
                                     ch_odd * 512 + 128 * j + 128],
                                bk[64:128, 128 * j:128 * j + 128])
                        ins.then_inc(S("sEPI"), 1)

            v.wait_ge(S("sPT2"), PT2_TOTAL)
            for tt in range(4):
                for e in range(NU):
                    zoff = F_Z + e * 4096 if e < NE else F_ZSH + (e - NE) * 4096
                    for ic in range(4):
                        cnt = FFN_IC_BEFORE[(tt, e, ic)]
                        v.wait_ge(S("sFA"), cnt + 1)
                        v.wait_ge(S("sFU"), cnt + 1)
                        if tt >= 1 and ic == 0:
                            v.wait_ge(S("sFD"), 8 * tt)
                        sig = V(F_SIG, F32, 512)
                        p1 = V(F_PS, F32, 512)
                        v.tensor_tensor(p1, sig, ps[1][:, 0:512],
                                        ALU.mult).then_inc(S("sFP"), 1)
                        v.wait_ge(S("sFP"), cnt + 1)
                        if e < NE:
                            v.wait_ge(S("sFM"), tt * NE + e + 1)
                            v.scalar_tensor_tensor(
                                V(zoff + ic * 1024, BF16, 512), p1,
                                wm_cols[:, e:e + 1], ps[2][:, 0:512],
                                ALU.mult, ALU.mult).then_inc(S("sFZ"), 1)
                        else:
                            v.tensor_copy(V(zoff + ic * 1024, BF16, 512),
                                          p1).then_inc(S("sFZ"), 1)
                for dc in range(8):
                    idxd = tt * 8 + dc
                    v.wait_ge(S("sFD"), idxd + 1)
                    v.tensor_copy(V(F_OFM + dc * 2048, F32, 512),
                                  ps[3][:, 0:512]).then_inc(S("sFE"), 1)
                for j in range(4):
                    idxr = tt * 4 + j
                    v.wait_ge(S("sFT"), idxr + 1)
                    v.wait_ge(S("sDXR"), 16 * (idxr + 1))
                    if idxr >= 1:
                        v.wait_ge(S("sDO"), 16 * idxr)
                    xr = V(F_XR, F32, 1024)
                    ob = V(F_OSB, F32, 1024)
                    v.tensor_tensor(ob[:, 0:512], ps[4][:, 0:512],
                                    xr[:, 0:512], ALU.add)
                    v.tensor_tensor(ob[:, 512:1024], ps[5][:, 0:512],
                                    xr[:, 512:1024], ALU.add).then_inc(S("sFO"), 1)

        @block.tensor
        def _(t_):
            t_.wait_ge(S("sGP"), 6)
            t_.wait_ge(S("sPR"), 1)

            def emit_qkT(tq):
                t_.wait_ge(S("sRO"), tq + 1)
                if tq >= 1:
                    t_.wait_ge(S("sQKE"), tq)
                qr = V(P_QR, BF16, 512)
                kr = V(P_KR, BF16, 256)
                ins = None
                for b in range(4):
                    ins = t_.transpose(ps6bf[:, 32 + 128 * b:160 + 128 * b],
                                       qr[:, 128 * b:128 * b + 128],
                                       ident_bf[:])
                for b in range(2):
                    ins = t_.transpose(ps6bf[:, 544 + 128 * b:672 + 128 * b],
                                       kr[:, 128 * b:128 * b + 128],
                                       ident_bf[:])
                ins.then_inc(S("sPT"), 1)

            def emit_stats(tq):
                t_.wait_ge(S("sRP"), tq + 1)
                t_.matmul(ps[4][0:1, 0:16], ones_c_f[:], V(P_GP, F32, 16),
                          start=(tq == 0), stop=(tq == NT - 1),
                          skip_group_check=True).then_inc(S("sST"), 1)

            for t in range(NT):
                pt = t % 2
                if t >= 2:
                    emit_stats(t - 2)
                if t >= 1:
                    emit_qkT(t - 1)
                t_.wait_ge(S("sXN"), 2 * t + 2)
                if t >= 1:
                    t_.wait_ge(S("sXEV"), t)
                xnb = V(P_XNB + pt * 2048, BF16, 1024)
                ins = None
                for fc in range(8):
                    ins = t_.transpose(psB[:, 128 * fc:128 * fc + 128],
                                       xnb[:, 128 * fc:128 * fc + 128],
                                       ident_bf[:])
                ins.then_inc(S("sPXT"), 1)
                t_.wait_ge(S("sXEV"), t + 1)
                if t >= 2:
                    t_.wait_ge(S("sRO"), t - 1)
                    t_.wait_ge(S("sGEX"), t - 1)
                if t == 0:
                    t_.wait_ge(S("sDC"), 16 * 2)
                    t_.wait_ge(S("sDW"), 64)
                    t_.wait_ge(S("sTL"), 1)
                fmv = V(P_FMT + pt * 2048, BF16, 1024)
                wqv = V(P_WQ, BF16, 8 * 512)
                wkv = V(P_WK, BF16, 8 * 512)
                pg = ps[6][:, 8 * pt:8 * pt + 8]
                for fc in range(8):
                    lhs = fmv[:, 128 * fc:128 * fc + 128]
                    t_.matmul(psq[pt][:, 0:512], lhs,
                              wqv[:, 512 * fc:512 * fc + 512],
                              start=(fc == 0), stop=(fc == 7),
                              skip_group_check=True)
                    t_.matmul(psq[pt][:, 512:1024], lhs,
                              wkv[:, 512 * fc:512 * fc + 512],
                              start=(fc == 0), stop=(fc == 7),
                              skip_group_check=True)
                    t_.matmul(pg, lhs, gwT[:, 8 * fc:8 * fc + 8],
                              start=(fc == 0), stop=False, skip_group_check=True)
                    t_.matmul(pg, lhs, gwT[:, 64 + 8 * fc:64 + 8 * fc + 8],
                              start=False, stop=False, skip_group_check=True)
                t_.matmul(pg, ones_r_f[:], lb_row[:],
                          start=False, stop=True,
                          skip_group_check=True).then_inc(S("sPQ"), 1)
            emit_stats(NT - 2)
            emit_qkT(NT - 1)
            emit_stats(NT - 1)

            t_.wait_ge(S("sRP"), NT)
            t_.wait_ge(S("sQKE"), NT)
            ins = None
            for j in range(2):
                ins = t_.transpose(psB[:, 128 * j:128 * j + 128],
                                   V(P_MBF, BF16, 256)[:, 128 * j:128 * j + 128],
                                   ident_bf[:])
            ins.then_inc(S("sDM"), 1)

            t_.wait_ge(S("sTL"), 5)
            t_.matmul(ps[5][:, 0:1], ones_r_f[:], gsc[0:1, 15:16],
                      start=True, stop=True, skip_group_check=True)
            t_.matmul(ps[5][:, 64:128], ones_r_f[:], gsc[0:1, 48:112],
                      start=True, stop=True, skip_group_check=True)
            t_.matmul(ps[5][:, 128:136], ones_r_f[:], gsc[0:1, 32:40],
                      start=True, stop=True,
                      skip_group_check=True).then_inc(S("sPT"), 1)

            pending_bt = []

            def emit_score(j):
                Tqj, rrj, halfj, schj, nsj = STEPS[j]
                parj = j % 2
                if j >= 1:
                    t_.wait_ge(S("sEX"), j - 1)
                if j < 2:
                    t_.wait_ge(S("sRP"), NT)
                    t_.wait_ge(S("sQKE"), NT)
                    t_.wait_ge(S("sGEX"), NT)
                kbj = rrj // 2
                q0 = 128 * max(schj - 4 * Tqj, 0)
                ins = None
                for u in range(2):
                    base = 64 * halfj + 32 * u
                    ins = t_.matmul(
                        SCB[(parj, u)][:, q0:512],
                        k_fm[base:base + 32,
                             kbj * 4096 + 128 * schj:
                             kbj * 4096 + 128 * schj + 128],
                        q_fm[base:base + 32,
                             rrj * 4096 + 512 * Tqj + q0:
                             rrj * 4096 + 512 * Tqj + 512],
                        start=True, stop=True, tile_position=(base, 0),
                        skip_group_check=True)
                ins.then_inc(S("sSC"), 1)

            emit_score(0)
            for k in range(NSTEP):
                Tq, rr, half, sch, ns = STEPS[k]
                par = k % 2
                R = STEP_ROUND[k]
                if k + 1 < NSTEP:
                    emit_score(k + 1)
                for Rbt in pending_bt:
                    t_.wait_ge(S("sSTG"), Rbt + 1)
                    stg = V(A_STG, F32, 4 * 128)
                    ins2 = None
                    for j in range(4):
                        ins2 = t_.transpose(
                            PVB[Rbt % 2][:, 128 * j:128 * j + 128],
                            stg[:, 128 * j:128 * j + 128], ident_f[:])
                    ins2.then_inc(S("sPT2"), 1)
                pending_bt = []
                t_.wait_ge(S("sEX"), k + 1)
                if is_diag(k):
                    t_.wait_ge(S("sMK"), MK_AFTER[k])
                if sch == 0:
                    if R >= 2:
                        t_.wait_ge(S("sSS"), R - 1)
                        t_.wait_ge(S("sEPI"), R - 1)
                    if R < 2:
                        t_.wait_ge(S("sTL"), 6)
                    if R == 1:
                        t_.wait_ge(S("sDM"), 2)
                ja, jb = RR_HEADS[rr]
                j_head = ja if half == 0 else jb
                gloc = j_head // 2
                st = (sch == 0)
                sp = (sch == ns - 1)
                q0 = 128 * max(sch - 4 * Tq, 0)
                banks = [PVB[R % 2], DNB2[R % 2]]
                ins = None
                for u in range(2):
                    psl = p_sb[:, (par * 2 + u) * 512 + q0:
                               (par * 2 + u) * 512 + 512]
                    ins = t_.matmul(banks[u][0:65, q0:512],
                                    v_tm[:, 260 * sch + 65 * gloc:
                                         260 * sch + 65 * gloc + 65],
                                    psl, start=st, stop=sp,
                                    skip_group_check=True)
                ins.then_inc(S("sPV"), 1)
                if sp:
                    t_.wait_ge(S("sEVC"), R + 1)
                    osb = V(A_OSB, F32, 512)
                    dsb = V(A_DSB, F32, 512)
                    ins = None
                    for j in range(4):
                        t_.transpose(PVB[R % 2][:, 128 * j:128 * j + 128],
                                     osb[:, 128 * j:128 * j + 128], ident_f[:])
                        ins = t_.transpose(DNB2[R % 2][:, 128 * j:128 * j + 128],
                                           dsb[:, 128 * j:128 * j + 128],
                                           ident_f[:])
                    ins.then_inc(S("sPT2"), 1)
                    if half == 1:
                        pending_bt.append(R)
            for Rbt in pending_bt:
                t_.wait_ge(S("sSTG"), Rbt + 1)
                stg = V(A_STG, F32, 4 * 128)
                ins2 = None
                for j in range(4):
                    ins2 = t_.transpose(
                        PVB[Rbt % 2][:, 128 * j:128 * j + 128],
                        stg[:, 128 * j:128 * j + 128], ident_f[:])
                ins2.then_inc(S("sPT2"), 1)

            if dbg:
                t_.wait_ge(S("sDBG"), 48)
            t_.wait_ge(S("sEX"), NSTEP)
            t_.wait_ge(S("sDW"), 80)
            t_.wait_ge(S("sDM2"), 144)
            t_.wait_ge(S("sCC"), 2)
            t_.wait_ge(S("sTL"), 6)
            t_.wait_ge(S("sDM3"), 32)
            t_.wait_ge(S("sEPI"), NROUND)
            for tt in range(4):
                for e in range(NU):
                    i_u = tt * NU + e
                    goff = F_SLG + (i_u % 2) * 8192
                    uoff = F_SLU + (i_u % 2) * 8192
                    gsl = V(goff, BF16, 4096)
                    usl = V(uoff, BF16, 4096)
                    t_.wait_ge(S("sDFF"), 16 * SDFF_AT[("gu", tt, e)])
                    if e < NE:
                        t_.wait_ge(S("sFZ"), FFN_E_BEFORE[(tt, e)])
                        t_.matmul(
                            ps[2][:, 0:512], ones_r_bf[:],
                            m_rows[0:1, 2048 * e + 512 * tt:
                                   2048 * e + 512 * tt + 512],
                            start=True, stop=True,
                            skip_group_check=True).then_inc(S("sFM"), 1)
                    for ic in range(4):
                        cnt = FFN_IC_BEFORE[(tt, e, ic)]
                        if cnt >= 1:
                            t_.wait_ge(S("sFA"), cnt)
                        ins = None
                        for fc in range(8):
                            ins = t_.matmul(
                                ps[0][:, 0:512],
                                gsl[:, fc * 512 + 128 * ic:
                                    fc * 512 + 128 * ic + 128],
                                xn_own[:, fc * 2048 + 512 * tt:
                                       fc * 2048 + 512 * tt + 512],
                                start=(fc == 0), stop=(fc == 7),
                                skip_group_check=True)
                        ins.then_inc(S("sFG"), 1)
                        if cnt >= 1:
                            t_.wait_ge(S("sFP"), cnt)
                        ins = None
                        for fc in range(8):
                            ins = t_.matmul(
                                ps[1][:, 0:512],
                                usl[:, fc * 512 + 128 * ic:
                                    fc * 512 + 128 * ic + 128],
                                xn_own[:, fc * 2048 + 512 * tt:
                                       fc * 2048 + 512 * tt + 512],
                                start=(fc == 0), stop=(fc == 7),
                                skip_group_check=True)
                        ins.then_inc(S("sFU"), 1)
                for dc in range(8):
                    idxd = tt * 8 + dc
                    t_.wait_ge(S("sDFF"), 16 * SDFF_AT[("dn", tt, dc)])
                    t_.wait_ge(S("sFZ"), FFN_TT_END[tt])
                    if idxd >= 1:
                        t_.wait_ge(S("sFE"), idxd)
                    dsl = V(F_SLD + (idxd % 2) * 8192, BF16, 8 * 512)
                    ssl = V(F_SDS, BF16, 8 * 128)
                    wsl = V(F_WOS, BF16, 8 * 128)
                    att = V(F_ATT, BF16, 8 * 512)
                    first = True
                    for hc in range(8):
                        t_.matmul(ps[3][:, 0:512],
                                  wsl[:, 128 * hc:128 * hc + 128],
                                  att[:, 512 * hc:512 * hc + 512],
                                  start=first, stop=False, skip_group_check=True)
                        first = False
                    for e in range(NE):
                        for ic in range(4):
                            t_.matmul(ps[3][:, 0:512],
                                      dsl[:, e * 512 + 128 * ic:
                                          e * 512 + 128 * ic + 128],
                                      V(F_Z + e * 4096 + ic * 1024, BF16, 512),
                                      start=False, stop=False,
                                      skip_group_check=True)
                    ins = None
                    for ic in range(8):
                        ins = t_.matmul(ps[3][:, 0:512],
                                        ssl[:, 128 * ic:128 * ic + 128],
                                        V(F_ZSH + ic * 1024, BF16, 512),
                                        start=False, stop=(ic == 7),
                                        skip_group_check=True)
                    ins.then_inc(S("sFD"), 1)
                for j in range(4):
                    idxr = tt * 4 + j
                    t_.wait_ge(S("sFE"), 8 * (tt + 1))
                    if idxr >= 1:
                        t_.wait_ge(S("sFO"), idxr)
                    ins = None
                    for dc in range(8):
                        tgt = ps[4] if dc < 4 else ps[5]
                        ins = t_.transpose(
                            tgt[:, 128 * (dc % 4):128 * (dc % 4) + 128],
                            V(F_OFM + dc * 2048, F32, 512)
                            [:, 128 * j:128 * j + 128],
                            ident_f[:])
                    ins.then_inc(S("sFT"), 1)

    return nc


def _rope_tables():
    inv_freq = 1.0 / 10000.0 ** (np.arange(0, HD, 2, dtype=np.float32) / HD)
    tt = np.arange(T, dtype=np.float32)
    freqs = np.outer(tt, inv_freq).astype(np.float32)
    cos = np.cos(freqs).astype(np.float32)
    sin = np.sin(freqs).astype(np.float32)
    cos2 = np.concatenate([cos, cos], 1)
    sin2 = np.concatenate([sin, -sin], 1)
    return (np.ascontiguousarray(np.tile(cos2, (1, 16))),
            np.ascontiguousarray(np.tile(sin2, (1, 16))))


ROPE_TABLES = _rope_tables()


def host_prepare(inputs, core):
    b, p = divmod(core, 2)
    x = np.ascontiguousarray(np.asarray(inputs["L0_attn_x"], np.float32)[b])
    Wq = np.asarray(inputs["L0_attn_Wq"], np.float32)
    Wk = np.asarray(inputs["L0_attn_Wk"], np.float32)
    Wv = np.asarray(inputs["L0_attn_Wv"], np.float32)
    qcols = []
    for rr in range(4):
        ja, jb = RR_HEADS[rr]
        for jh in (ja, jb):
            hgl = 8 * p + jh
            for c in range(2):
                m = 2 * hgl + c
                qcols.extend(range(32 * m, 32 * m + 32))
    egw = np.asarray(inputs["L0_ffn_expert_gate_w"], np.float32)
    euw = np.asarray(inputs["L0_ffn_expert_up_w"], np.float32)
    edw = np.asarray(inputs["L0_ffn_expert_down_w"], np.float32)
    sgw = np.asarray(inputs["L0_ffn_shared_gate_w"], np.float32)
    suw = np.asarray(inputs["L0_ffn_shared_up_w"], np.float32)
    sdw = np.asarray(inputs["L0_ffn_shared_down_w"], np.float32)
    return dict(
        x4096=x,
        x_own=np.ascontiguousarray(x[2048 * p:2048 * p + 2048]),
        wq_eff=np.ascontiguousarray(Wq[:, qcols]),
        wkv_eff=np.ascontiguousarray(np.concatenate(
            [Wk[:, 256 * p:256 * p + 256], Wv[:, 256 * p:256 * p + 256]], 1)),
        wo_in=np.ascontiguousarray(np.asarray(inputs["L0_attn_Wo"], np.float32)),
        lamv=np.concatenate([
            np.asarray(inputs["L0_attn_lambda_q1"], np.float32),
            np.asarray(inputs["L0_attn_lambda_k1"], np.float32),
            np.asarray(inputs["L0_attn_lambda_q2"], np.float32),
            np.asarray(inputs["L0_attn_lambda_k2"], np.float32)]).reshape(1, 128),
        rmsw_in=np.asarray(inputs["L0_attn_rms_w"], np.float32).reshape(1, 64),
        gwt_in=np.ascontiguousarray(
            np.asarray(inputs["L0_ffn_gate_w"], np.float32).T),
        lbb=np.asarray(inputs["L0_ffn_lb_bias"], np.float32).reshape(1, 8),
        egw_t=np.ascontiguousarray(egw.transpose(0, 2, 1)),
        euw_t=np.ascontiguousarray(euw.transpose(0, 2, 1)),
        edw_t2=np.ascontiguousarray(
            edw.transpose(0, 2, 1).reshape(NE, 512, 8, 128).transpose(0, 2, 1, 3)),
        sgw_t=np.ascontiguousarray(sgw.T),
        suw_t=np.ascontiguousarray(suw.T),
        sdw_t2=np.ascontiguousarray(
            sdw.T.reshape(1024, 8, 128).transpose(1, 0, 2)),
        cosr=ROPE_TABLES[0], sinr=ROPE_TABLES[1],
    )


def kernel(**inputs):
    nc = build_nc()
    in_maps = [host_prepare(inputs, c) for c in range(8)]
    res = run_bass_kernel_spmd(nc, in_maps, core_ids=list(range(8)))
    out = np.empty((4, T, E), np.float32)
    for c in range(8):
        b, p = divmod(c, 2)
        out[b, 2048 * p:2048 * p + 2048] = res.results[c]["out_d"]
    return out
